# version 10 (final)
# baseline (speedup 1.0000x reference)
from contextlib import ExitStack

import numpy as np

import concourse.bass as bass
import concourse.mybir as mybir
from concourse.bass_utils import run_bass_kernel_spmd

F32 = mybir.dt.float32
BF16 = mybir.dt.bfloat16
AF = mybir.ActivationFunctionType
ALU = mybir.AluOpType
AX = mybir.AxisListType

T = 2048
E = 2048
H = 16
D = 128
FQKV = 3 * E
FI = 6464
TT = T // 128
NIC = 51
NC = 8
LN_EPS = 1e-5
QEPS = 1e-5

_DEBUG = {}
_P3_FO_LIMIT = 12
_HQ_NOPUMP = False

ENGS = ("sync", "tensor", "vector", "scalar", "gpsimd")


class Sem:
    __slots__ = ("name", "handle", "count")

    def __init__(self, name):
        self.name = name
        self.handle = None
        self.count = 0


class Op:
    __slots__ = ("fn", "waits", "incs", "seq", "tag")

    def __init__(self, fn, seq, tag):
        self.fn = fn
        self.waits = []
        self.incs = []
        self.seq = seq
        self.tag = tag


class KB:

    def __init__(self, nc):
        self.nc = nc
        self.streams = {e: [] for e in ENGS}
        self.sems = {}
        self.seq = 0

    def sem(self, name):
        if name not in self.sems:
            self.sems[name] = Sem(name)
        return self.sems[name]

    def op(self, eng, fn, tag="", waits=()):
        o = Op(fn, self.seq, tag)
        self.seq += 1
        self.streams[eng].append(o)
        for ev in waits:
            self.wait(o, ev)
        return o

    @staticmethod
    def wait(o, ev):
        if ev is None:
            return o
        sem, val, pseq = ev
        assert pseq < o.seq, f"wait on future event {sem.name}@{val} at {o.tag}"
        for i, (s2, v2, _) in enumerate(o.waits):
            if s2 is sem:
                if val > v2:
                    o.waits[i] = (sem, val, pseq)
                return o
        o.waits.append((sem, val, pseq))
        return o

    @staticmethod
    def inc(o, sem, n=1):
        sem.count += n
        o.incs.append((sem, n, sem.count))
        return (sem, sem.count, o.seq)

    def dma(self, queue_eng, out_ap_fn, in_ap_fn, tag="dma", waits=(), ch=None):
        sem = self.sem("d_" + (ch or queue_eng))
        prev = (sem, sem.count, -1) if sem.count else None

        def fn(eng):
            return eng.dma_start(out=out_ap_fn(), in_=in_ap_fn())

        o = self.op(queue_eng, fn, tag, waits)
        if prev is not None:
            o.waits.append((sem, prev[1], o.seq - 1))
        ev = self.inc(o, sem, 16)
        return o, ev

    def check_inc_order(self):
        seen = {}
        for eng, stream in self.streams.items():
            last_val = {}
            for o in stream:
                for (sem, n, val) in o.incs:
                    prev_eng = seen.setdefault(sem.name, eng)
                    assert prev_eng == eng, \
                        f"sem {sem.name} inc'd from {prev_eng} and {eng}"
                    lv = last_val.get(sem.name)
                    assert lv is None or val > lv, \
                        f"sem {sem.name} inc value {val} at {o.tag} is " \
                        f"stream-later than value {lv} (out-of-order inc)"
                    last_val[sem.name] = val

    def emit(self):
        self.check_inc_order()
        nc = self.nc
        with ExitStack() as es:
            for s in self.sems.values():
                s.handle = es.enter_context(nc.semaphore(s.name))
            block = es.enter_context(nc.Block())

            def render(stream):
                def runner(eng):
                    for o in stream:
                        for (sem, val, _) in o.waits:
                            eng.wait_ge(sem.handle, val)
                        ins = o.fn(eng)
                        for (sem, n, _v) in o.incs:
                            assert ins is not None, f"inc w/o instruction: {o.tag}"
                            ins = ins.then_inc(sem.handle, n)
                return runner

            for ename in ENGS:
                if self.streams[ename]:
                    getattr(block, ename)(render(self.streams[ename]))


def _ts(i, n=128):
    return slice(i * n, (i + 1) * n)


def ceil_div(a, b):
    return -(-a // b)


def _ap3(t2d, off, dim1, dim2):
    a = t2d[:]
    return bass.AP(a.tensor, a.offset + off, [list(a.ap[0]), list(dim1), list(dim2)])


def host_constants():
    import ml_dtypes
    c = {}
    import jax
    with jax.default_device(jax.devices("cpu")[0]):
        import jax.numpy as jnp
        inv = 1.0 / 10000.0 ** (jnp.arange(0, D, 2, dtype=jnp.float32) / D)
        t32 = jnp.arange(T, dtype=jnp.float32)[:, None] * inv[None, :]
        c["cos_hi"] = np.ascontiguousarray(np.asarray(jnp.cos(t32), np.float32))
        c["sin_hi"] = np.ascontiguousarray(np.asarray(jnp.sin(t32), np.float32))
    c["cos_lo"] = np.zeros_like(c["cos_hi"])
    c["sin_lo"] = np.zeros_like(c["sin_hi"])
    c["ident_f32"] = np.eye(128, dtype=np.float32)
    c["ident_bf16"] = np.eye(128, dtype=ml_dtypes.bfloat16)
    c["trimask"] = (np.arange(128)[None, :] >= np.arange(128)[:, None]).astype(np.float32)
    return c



def emit_peel(kb, sems, zb, ubuf, cbuf_, outw, off, width, ev_z, tag,
              extra_waits=(), bias_m128=None):
    sem_v, sem_a, sem_g = sems
    w1 = min(704, max(64, (width // 3) // 64 * 64))
    if width <= 256:
        w1 = width
    rest = width - w1
    wa = (rest // 2) // 64 * 64
    wb = rest - wa
    chains = [("dve", off, w1)]
    if wa:
        chains.append(("agsA", off + w1, wa))
    if wb:
        chains.append(("agsB", off + w1 + wa, wb))
    done = []
    for nm, o0, w in chains:
        if nm != "dve":
            continue
        o = kb.op("vector", lambda eng, o0=o0, w=w: eng.tensor_copy(
            ubuf[:, o0:o0 + w], zb[:, o0:o0 + w]), f"{tag}u",
            [ev_z] + list(extra_waits))
        ev_u = kb.inc(o, sem_v)
        for k in range(7, -1, -1):
            Tk = float(1 << k)
            o = kb.op("vector", lambda eng, Tk=Tk, o0=o0, w=w: eng.tensor_scalar(
                cbuf_[:, o0:o0 + w], ubuf[:, o0:o0 + w], Tk, None, ALU.is_ge),
                f"{tag}c{k}", [ev_u])
            ev_c = kb.inc(o, sem_v)
            o = kb.op("vector", lambda eng, Tk=Tk, o0=o0, w=w:
                      eng.scalar_tensor_tensor(
                          ubuf[:, o0:o0 + w], cbuf_[:, o0:o0 + w], -Tk,
                          ubuf[:, o0:o0 + w], ALU.mult, ALU.add),
                      f"{tag}p{k}", [ev_c])
            ev_u = kb.inc(o, sem_v)
        o = kb.op("vector", lambda eng, o0=o0, w=w: eng.tensor_tensor(
            cbuf_[:, o0:o0 + w], zb[:, o0:o0 + w], ubuf[:, o0:o0 + w],
            ALU.subtract), f"{tag}q1", [ev_u])
        ev = kb.inc(o, sem_v)
        o = kb.op("vector", lambda eng, o0=o0, w=w: eng.tensor_scalar(
            outw(o0, w), cbuf_[:, o0:o0 + w], -128.0, None, ALU.add),
            f"{tag}q2", [ev])
        done.append(kb.inc(o, sem_v))
    ags = [(nm, o0, w) for nm, o0, w in chains if nm != "dve"]
    ev_v = {}
    for nm, o0, w in ags:
        o = kb.op("vector", lambda eng, o0=o0, w=w: eng.tensor_copy(
            ubuf[:, o0:o0 + w], zb[:, o0:o0 + w]), f"{tag}{nm}v",
            [ev_z] + list(extra_waits))
        ev_v[nm] = kb.inc(o, sem_v)
    bias_t, bias_ev = bias_m128
    for k in range(7, -1, -1):
        Tk = float(1 << k)
        ev_s = {}
        for nm, o0, w in ags:
            o = kb.op("scalar", lambda eng, o0=o0, w=w:
                      eng.activation(cbuf_[:, o0:o0 + w], ubuf[:, o0:o0 + w],
                                     AF.Sign, bias=bias_t[:, 0:1]), f"{tag}{nm}s{k}",
                      [ev_v[nm], bias_ev if k == 7 else None])
            ev_s[nm] = kb.inc(o, sem_a)
        for nm, o0, w in ags:
            o = kb.op("vector", lambda eng, o0=o0, w=w, Tk=Tk:
                      eng.scalar_tensor_tensor(
                          ubuf[:, o0:o0 + w], cbuf_[:, o0:o0 + w], -Tk / 2,
                          ubuf[:, o0:o0 + w], ALU.mult, ALU.add),
                      f"{tag}{nm}p{k}", [ev_s[nm]])
            ev_v[nm] = kb.inc(o, sem_v)
    for nm, o0, w in ags:
        o = kb.op("vector", lambda eng, o0=o0, w=w: eng.tensor_tensor(
            cbuf_[:, o0:o0 + w], zb[:, o0:o0 + w], ubuf[:, o0:o0 + w],
            ALU.subtract), f"{tag}{nm}q1", [ev_v[nm]])
        ev = kb.inc(o, sem_v)
        o = kb.op("vector", lambda eng, o0=o0, w=w: eng.tensor_scalar(
            outw(o0, w), cbuf_[:, o0:o0 + w], -0.5, None, ALU.add),
            f"{tag}{nm}q2", [ev])
        done.append(kb.inc(o, sem_v))
    return done



def build_nc(debug=()):
    nc = bass.Bass("TRN2", target_bir_lowering=False, debug=False,
                   detect_race_conditions=False)
    kb = KB(nc)
    dbg = set(debug)

    x_in = nc.dram_tensor("x", [T, E], F32, kind="ExternalInput").ap()
    w_qkv = nc.dram_tensor("w_qkv", [E, FQKV], F32, kind="ExternalInput").ap()
    w_out = nc.dram_tensor("w_out", [E, E], F32, kind="ExternalInput").ap()
    ln_sc = nc.dram_tensor("ln_scale_b", [128, 128], F32, kind="ExternalInput").ap()
    ln_bi = nc.dram_tensor("ln_bias_b", [128, 128], F32, kind="ExternalInput").ap()
    w_g = nc.dram_tensor("w_gate", [FI, E], F32, kind="ExternalInput").ap()
    w_u = nc.dram_tensor("w_up", [FI, E], F32, kind="ExternalInput").ap()
    w_d = nc.dram_tensor("w_down", [E, FI], F32, kind="ExternalInput").ap()
    cos_hi = nc.dram_tensor("cos_hi", [T, 64], F32, kind="ExternalInput").ap()
    cos_lo = nc.dram_tensor("cos_lo", [T, 64], F32, kind="ExternalInput").ap()
    sin_hi = nc.dram_tensor("sin_hi", [T, 64], F32, kind="ExternalInput").ap()
    sin_lo = nc.dram_tensor("sin_lo", [T, 64], F32, kind="ExternalInput").ap()
    ident_f = nc.dram_tensor("ident_f32", [128, 128], F32, kind="ExternalInput").ap()
    ident_b = nc.dram_tensor("ident_bf16", [128, 128], BF16, kind="ExternalInput").ap()
    trimask = nc.dram_tensor("trimask", [128, 128], F32, kind="ExternalInput").ap()

    out = nc.dram_tensor("out", [T, E], F32, kind="ExternalOutput").ap()

    s_xnfm_hi = nc.dram_tensor("s_xnfm_hi", [TT, 128, E], BF16).ap()
    s_xnfm_lo = nc.dram_tensor("s_xnfm_lo", [TT, 128, E], BF16).ap()
    s_xq = nc.dram_tensor("s_xq", [TT, 128, E], BF16).ap()
    s_q_hi = nc.dram_tensor("s_q_hi", [H, 128, T], BF16).ap()
    s_q_lo = nc.dram_tensor("s_q_lo", [H, 128, T], BF16).ap()
    s_k_hi = nc.dram_tensor("s_k_hi", [H, 128, T], BF16).ap()
    s_k_lo = nc.dram_tensor("s_k_lo", [H, 128, T], BF16).ap()
    s_v_hi = nc.dram_tensor("s_v_hi", [H, 128, T], BF16).ap()
    s_v_lo = nc.dram_tensor("s_v_lo", [H, 128, T], BF16).ap()
    s_at_hi = nc.dram_tensor("s_at_hi", [TT, 128, E], BF16).ap()
    s_at_lo = nc.dram_tensor("s_at_lo", [TT, 128, E], BF16).ap()
    s_ya = nc.dram_tensor("s_ya", [T, E], F32).ap()
    s_h = nc.dram_tensor("s_h", [T, FI], F32).ap()
    s_hq = nc.dram_tensor("s_hq", [TT, 128, NIC * 128], BF16).ap()

    dbg_aps = {}

    def dbg_out(name, shape):
        dbg_aps[name] = nc.dram_tensor(name, shape, F32, kind="ExternalOutput").ap()
        return dbg_aps[name]

    es = ExitStack()
    e = es.enter_context

    identf = e(nc.sbuf_tensor("identf", [128, 128], F32))
    identb = e(nc.sbuf_tensor("identb", [128, 128], BF16))
    trim = e(nc.sbuf_tensor("trim", [128, 128], F32))
    lnscb = e(nc.sbuf_tensor("lnscb", [128, 128], F32))
    lnbib = e(nc.sbuf_tensor("lnbib", [128, 128], F32))
    cosb = e(nc.sbuf_tensor("cosb", [128, TT * 64], F32))
    sinb = e(nc.sbuf_tensor("sinb", [128, TT * 64], F32))
    ones_k = e(nc.sbuf_tensor("ones_k", [128, 1], BF16))
    ones_f = e(nc.sbuf_tensor("ones_f", [128, 1], F32))
    ones_row = e(nc.sbuf_tensor("ones_row", [1, 128], F32))
    biasm128 = e(nc.sbuf_tensor("biasm128", [128, 1], F32))
    sxr_all = e(nc.sbuf_tensor("sxr_all", [128, TT], F32))
    csx_all = e(nc.sbuf_tensor("csx_all", [128, TT], F32))
    sxh_all = e(nc.sbuf_tensor("sxh_all", [128, TT], F32))
    csh_all = e(nc.sbuf_tensor("csh_all", [128, TT], F32))
    hmax_all = e(nc.sbuf_tensor("hmax_all", [128, TT], F32))
    cdxp = e(nc.sbuf_tensor("cdxp", [128, TT], F32))
    wq_sc = {nm: {k: e(nc.sbuf_tensor(f"{k}_{nm}", [128, 1], F32))
                  for k in ("cw", "sw")} for nm in ("g", "u", "d")}

    sem_t = kb.sem("tsem")
    sem_v = kb.sem("vsem")
    sem_a = kb.sem("asem")
    sem_g = kb.sem("gsem")

    ev_const = {}
    for nm, dst, src in (("identf", identf, ident_f), ("identb", identb, ident_b),
                         ("trim", trim, trimask), ("lnsc", lnscb, ln_sc),
                         ("lnbi", lnbib, ln_bi)):
        _, ev = kb.dma("sync", lambda d=dst: d[:], lambda s=src: s[:], "c_" + nm, ch="cmisc")
        ev_const[nm] = ev
    ropetmp = e(nc.sbuf_tensor("ropetmp", [128, TT * 64], F32))
    ev_prev_radd = None
    for nm, dstb, srch, srcl in (("cos", cosb, cos_hi, cos_lo),
                                 ("sin", sinb, sin_hi, sin_lo)):
        evs = []
        for ti, t in enumerate(range(TT)):
            _, ev1 = kb.dma("sync", lambda d=dstb, t=t: d[:, t * 64:(t + 1) * 64],
                            lambda s=srch, t=t: s[_ts(t), :], "rh", ch=f"rope{t % 4}")
            _, ev2 = kb.dma("sync", lambda t=t: ropetmp[:, t * 64:(t + 1) * 64],
                            lambda s=srcl, t=t: s[_ts(t), :], "rl",
                            waits=[ev_prev_radd] if (ti == 0 and ev_prev_radd) else (),
                            ch=f"rope{t % 4}")
            evs += [ev1, ev2]
        o = kb.op("vector", lambda eng, d=dstb: eng.tensor_tensor(
            d[:], d[:], ropetmp[:], ALU.add), "radd_" + nm, evs)
        ev_const[nm] = ev_prev_radd = kb.inc(o, sem_v)
    o = kb.op("vector", lambda eng: eng.memset(ones_k[:], 1.0), "ones_k")
    ev_const["ones_k"] = kb.inc(o, sem_v)
    o = kb.op("vector", lambda eng: eng.memset(ones_f[:], 1.0), "ones_f")
    ev_const["ones_f"] = kb.inc(o, sem_v)
    o = kb.op("vector", lambda eng: eng.memset(ones_row[:], 1.0), "ones_row")
    ev_const["ones_row"] = kb.inc(o, sem_v)
    o = kb.op("vector", lambda eng: eng.memset(biasm128[:], -128.0), "bm128")
    ev_const["bm128"] = kb.inc(o, sem_v)

    es12 = ExitStack()
    e12 = es12.enter_context
    wsum = e12(nc.sbuf_tensor("wsum", [128, 3], F32))
    wslab = [e12(nc.sbuf_tensor(f"wslab{i}", [128, 8192], F32)) for i in range(2)]
    wred = e12(nc.sbuf_tensor("wred", [128, 1], F32))
    msc = e12(nc.sbuf_tensor("msc", [1, 8], F32))
    ps_m = e12(nc.psum_tensor("ps_m", [128, 512], F32))

    o = kb.op("vector", lambda eng: eng.memset(wsum[:], 0.0), "p1z")
    ev_p1z = kb.inc(o, sem_v)
    p1_specs = [("g", w_g, FI, E), ("u", w_u, FI, E), ("d", w_d, E, FI)]
    p1_jobs = []
    for wi, (nm, wap, rows, cols) in enumerate(p1_specs):
        for sidx in range(ceil_div(rows, 128)):
            p1_jobs.append((nm, wi, wap, sidx * 128,
                            min(128, rows - sidx * 128), cols))
    p1_state = {"i": 0, "free": [None, None], "mean_evs": {}}

    def p1_pump(n):
        for _ in range(n):
            i = p1_state["i"]
            if i >= len(p1_jobs):
                return
            nm, wi, wap, r0, nr, cols = p1_jobs[i]
            buf = i % 2
            fr = p1_state["free"][buf]
            _, ev_l = kb.dma("sync",
                             lambda b=buf, nr=nr, cols=cols: wslab[b][:nr, :cols],
                             lambda w=wap, r0=r0, nr=nr: w[r0:r0 + nr, :],
                             f"p1l_{nm}{r0}",
                             waits=[fr] if fr else (), ch=f"wsl{buf}")
            o = kb.op("vector", lambda eng, b=buf, nr=nr, cols=cols: eng.reduce_sum(
                wred[:nr, :], wslab[b][:nr, :cols], AX.X,
                apply_absolute_value=True), f"p1r_{nm}{r0}", [ev_l, ev_p1z])
            ev = kb.inc(o, sem_v)
            o = kb.op("vector", lambda eng, nr=nr, wi=wi: eng.tensor_tensor(
                wsum[:nr, wi:wi + 1], wsum[:nr, wi:wi + 1], wred[:nr, :], ALU.add),
                f"p1a_{nm}{r0}", [ev])
            p1_state["free"][buf] = kb.inc(o, sem_v)
            p1_state["mean_evs"][nm] = p1_state["free"][buf]
            p1_state["i"] += 1

    xb = [e12(nc.sbuf_tensor(f"xb{i}", [128, E], F32)) for i in range(2)]
    xnb = e12(nc.sbuf_tensor("xnb", [128, E], F32))
    sqb = e12(nc.sbuf_tensor("sqb", [128, E], F32))
    st1 = e12(nc.sbuf_tensor("st1", [128, 8], F32))
    zb = e12(nc.sbuf_tensor("zb", [128, E], F32))
    ub = e12(nc.sbuf_tensor("ub", [128, E], F32))
    cbuf = e12(nc.sbuf_tensor("cbuf", [128, E], F32))
    xqb = e12(nc.sbuf_tensor("xqb", [128, E], BF16))
    tstg_hi = e12(nc.sbuf_tensor("tstg_hi", [128, E], BF16))
    tstg_lo = e12(nc.sbuf_tensor("tstg_lo", [128, E], BF16))
    tstg_xq = e12(nc.sbuf_tensor("tstg_xq", [128, E], BF16))
    ps_t = [e12(nc.psum_tensor(f"ps_t{i}", [128, 512], F32)) for i in range(2)]
    ps_tb = [e12(nc.psum_tensor(f"ps_tb{i}", [128, 1024], BF16)) for i in range(2)]

    xb_free = [None, None]
    bar = {"v": None, "a": None, "t": None, "st": None}
    ev_p2_store = [None] * TT
    ev_sxr_t = [None] * TT
    for t in range(TT):
        buf = t % 2
        _, ev_x = kb.dma("sync", lambda b=buf: xb[b][:],
                         lambda t=t: x_in[_ts(t), :], f"p2x{t}",
                         waits=[xb_free[buf]] if xb_free[buf] else (),
                         ch=f"x{buf}")
        tile_bar = [bar["v"], bar["a"], bar["t"], bar["st"]]
        o = kb.op("vector", lambda eng, b=buf: eng.reduce_sum(
            st1[:, 0:1], xb[b][:], AX.X), f"p2m{t}", [ev_x] + tile_bar)
        ev = kb.inc(o, sem_v)
        o = kb.op("vector", lambda eng: eng.tensor_scalar(
            st1[:, 1:2], st1[:, 0:1], 1.0 / E, None, ALU.mult), f"p2m2{t}", [ev])
        ev_mean = kb.inc(o, sem_v)
        o = kb.op("scalar", lambda eng, b=buf: eng.activation(
            sqb[:], xb[b][:], AF.Square, accum_out=st1[:, 2:3]), f"p2sq{t}",
            [ev_x] + tile_bar)
        ev_sq = kb.inc(o, sem_a)
        o = kb.op("vector", lambda eng: eng.tensor_tensor(
            st1[:, 3:4], st1[:, 1:2], st1[:, 1:2], ALU.mult), f"p2v1{t}", [ev_mean])
        ev = kb.inc(o, sem_v)
        o = kb.op("vector", lambda eng: eng.scalar_tensor_tensor(
            st1[:, 4:5], st1[:, 2:3], 1.0 / E, st1[:, 3:4], ALU.mult, ALU.subtract),
            f"p2v2{t}", [ev, ev_sq])
        ev = kb.inc(o, sem_v)
        o = kb.op("vector", lambda eng: eng.tensor_scalar(
            st1[:, 5:6], st1[:, 4:5], LN_EPS, None, ALU.add), f"p2v3{t}", [ev])
        ev = kb.inc(o, sem_v)
        o = kb.op("vector", lambda eng: eng.reciprocal(
            st1[:, 6:7], st1[:, 5:6]), f"p2v4{t}", [ev])
        ev = kb.inc(o, sem_v)
        o = kb.op("scalar", lambda eng: eng.activation(
            st1[:, 7:8], st1[:, 6:7], AF.Sqrt), f"p2v5{t}", [ev])
        ev_rstd = kb.inc(o, sem_a)
        o = kb.op("vector", lambda eng, b=buf: eng.tensor_scalar(
            xnb[:], xb[b][:], st1[:, 1:2], st1[:, 7:8], ALU.subtract, ALU.mult),
            f"p2xn{t}", [ev_rstd])
        ev_xn = kb.inc(o, sem_v)
        xb_free[buf] = ev_xn
        o = kb.op("vector", lambda eng: eng.reduce_max(
            st1[:, 0:1], xnb[:], AX.X, apply_absolute_value=True), f"p2am{t}", [ev_xn])
        ev = kb.inc(o, sem_v)
        o = kb.op("vector", lambda eng: eng.tensor_scalar(
            st1[:, 1:2], st1[:, 0:1], QEPS, None, ALU.max), f"p2am2{t}", [ev])
        ev = kb.inc(o, sem_v)
        o = kb.op("vector", lambda eng: eng.reciprocal(st1[:, 2:3], st1[:, 1:2]),
                  f"p2am3{t}", [ev])
        ev = kb.inc(o, sem_v)
        o = kb.op("vector", lambda eng, t=t: eng.tensor_scalar(
            sxr_all[:, t:t + 1], st1[:, 2:3], 127.0, None, ALU.mult), f"p2am4{t}", [ev])
        ev_sxr = kb.inc(o, sem_v)
        ev_sxr_t[t] = ev_sxr
        o = kb.op("vector", lambda eng, t=t: eng.reciprocal(
            csx_all[:, t:t + 1], sxr_all[:, t:t + 1]), f"p2am5{t}", [ev_sxr])
        kb.inc(o, sem_v)
        o = kb.op("scalar", lambda eng, t=t: eng.activation(
            zb[:], xnb[:], AF.Copy, bias=128.5, scale=sxr_all[:, t:t + 1]),
            f"p2z{t}", [ev_sxr])
        ev_z = kb.inc(o, sem_a)
        peel_done = emit_peel(
            kb, (sem_v, sem_a, sem_g), zb, ub, cbuf,
            lambda o0, w: xqb[:, o0:o0 + w], 0, E, ev_z, f"p2pl{t}_",
            extra_waits=tile_bar, bias_m128=(biasm128, ev_const["bm128"]))
        o = kb.op("vector", lambda eng: eng.engine_nop(), f"p2xqn{t}", peel_done)
        ev_xq = kb.inc(o, sem_v)

        pe_ev = {0: None, 1: None}
        pe_evb = {0: None, 1: None}
        last_v = ev_xq
        for j in range(16):
            pb = j % 2
            o = kb.op("tensor", lambda eng, j=j, pb=pb: eng.transpose(
                ps_t[pb][:, 0:128], xnb[:, _ts(j)], identf[:]), f"p2t{t}_{j}",
                [ev_xn, ev_const["identf"], pe_ev[pb]] + (tile_bar if j < 2 else []))
            ev_tp = kb.inc(o, sem_t)
            o = kb.op("vector", lambda eng, j=j, pb=pb: eng.tensor_copy(
                tstg_hi[:, _ts(j)], ps_t[pb][:, 0:128]), f"p2th{t}_{j}", [ev_tp])
            ev = kb.inc(o, sem_v)
            o = kb.op("vector", lambda eng, j=j, pb=pb: eng.tensor_tensor(
                tstg_lo[:, _ts(j)], ps_t[pb][:, 0:128], tstg_hi[:, _ts(j)],
                ALU.subtract), f"p2tl{t}_{j}", [ev])
            pe_ev[pb] = kb.inc(o, sem_v)
            last_v = pe_ev[pb]
        last_xq_tp = None
        for j in range(16):
            pb = j % 2
            o = kb.op("tensor", lambda eng, j=j, pb=pb: eng.transpose(
                ps_tb[pb][:, 0:128], xqb[:, _ts(j)], identb[:]), f"p2tq{t}_{j}",
                [ev_xq, ev_const["identb"], pe_evb[pb]])
            ev_tp = kb.inc(o, sem_t)
            last_xq_tp = ev_tp
            o = kb.op("vector", lambda eng, j=j, pb=pb: eng.tensor_copy(
                tstg_xq[:, _ts(j)], ps_tb[pb][:, 0:128]), f"p2tqe{t}_{j}", [ev_tp])
            pe_evb[pb] = kb.inc(o, sem_v)
            last_v = pe_evb[pb]

        evs_st = []
        for ki, (stg, tgt) in enumerate(((tstg_hi, s_xnfm_hi), (tstg_lo, s_xnfm_lo),
                                         (tstg_xq, s_xq))):
            _, ev_s = kb.dma("gpsimd", lambda tg=tgt, t=t: tg[t],
                             lambda s=stg: s[:], f"p2st{t}",
                             waits=[last_v], ch=f"st2{ki}")
            evs_st.append(ev_s)
        ev_p2_store[t] = evs_st
        bar = {"v": last_v, "a": ev_z, "t": last_xq_tp, "st": evs_st[-1]}

        p1_pump(8)
        if "xn" in dbg:
            ap = dbg_aps.get("dbg_xn") or dbg_out("dbg_xn", [T, E])
            kb.dma("gpsimd", lambda t=t, a=ap: a[_ts(t), :], lambda: xnb[:],
                   f"dxn{t}", waits=[last_v], ch="dbg")
        if "xq" in dbg:
            ap = dbg_aps.get("dbg_xq") or dbg_out("dbg_xq", [T, E])
            o, _ = kb.dma("gpsimd", lambda t=t, a=ap: a[_ts(t), :], lambda: cbuf[:],
                          f"dxq{t}", waits=[ev_xq], ch="dbg")

    p1_pump(len(p1_jobs))
    counts = {"g": FI * E, "u": FI * E, "d": E * FI}
    for wi, nm in enumerate(("g", "u", "d")):
        o = kb.op("tensor", lambda eng, wi=wi: eng.matmul(
            ps_m[:1, 0:1], wsum[:, wi:wi + 1], ones_f[:], start=True, stop=True),
            f"p1mm_{nm}", [p1_state["mean_evs"][nm], ev_const["ones_f"]])
        if wi > 0:
            kb.wait(o, ev_cp)
        ev_mm = kb.inc(o, sem_t)
        o = kb.op("vector", lambda eng, wi=wi: eng.tensor_copy(
            msc[:1, wi:wi + 1], ps_m[:1, 0:1]), f"p1c_{nm}", [ev_mm])
        ev_cp = kb.inc(o, sem_v)
    o = kb.op("vector", lambda eng: eng.tensor_scalar(
        msc[:1, 3:6], msc[:1, 0:3], 1.0 / counts["g"], QEPS, ALU.mult, ALU.max),
        "p1s", [ev_cp])
    ev = kb.inc(o, sem_v)
    o = kb.op("vector", lambda eng: eng.reciprocal(msc[:1, 6:7], msc[:1, 3:4]), "p1r0", [ev])
    ev = kb.inc(o, sem_v)
    ev_wsc = {}
    ev_prev_cp = ev
    for wi, nm in enumerate(("g", "u", "d")):
        o = kb.op("vector", lambda eng, wi=wi: eng.reciprocal(
            msc[:1, 6:7], msc[:1, 3 + wi:4 + wi]), f"p1rc_{nm}", [ev_prev_cp])
        ev_r = kb.inc(o, sem_v)
        for key, col in (("cw", 3 + wi), ("sw", 6)):
            o = kb.op("tensor", lambda eng, col=col: eng.matmul(
                ps_m[:, 0:1], ones_row[:1, :], msc[:1, col:col + 1],
                start=True, stop=True), f"p1b_{nm}{key}",
                [ev_r, ev_const["ones_row"]])
            if ev_wsc:
                kb.wait(o, last_bc_cp)
            ev_b = kb.inc(o, sem_t)
            o = kb.op("vector", lambda eng, nm=nm, key=key: eng.tensor_copy(
                wq_sc[nm][key][:], ps_m[:, 0:1]), f"p1bc_{nm}{key}", [ev_b])
            last_bc_cp = kb.inc(o, sem_v)
            ev_wsc[(nm, key)] = last_bc_cp
        ev_prev_cp = last_bc_cp
    ev_p1_done = last_bc_cp

    if "means" in dbg:
        ap = dbg_out("dbg_means", [1, 8])
        kb.dma("gpsimd", lambda a=ap: a[:, :], lambda: msc[:, :], "dmeans",
               waits=[ev_p1_done], ch="dbg")

    es12.close()
    p2bar = [bar["v"], bar["a"], bar["t"], bar["st"], ev_p1_done]

    es5a = ExitStack()
    e5a = es5a.enter_context
    xqr = [e5a(nc.sbuf_tensor(f"xqr{i}", [128, E], BF16)) for i in range(TT)]
    wslab5 = e5a(nc.sbuf_tensor("wslab5", [128, E], F32))
    absb = e5a(nc.sbuf_tensor("absb", [128, E], F32))
    sgnb = e5a(nc.sbuf_tensor("sgnb", [128, E], F32))
    mskb = e5a(nc.sbuf_tensor("mskb", [128, E], F32))
    wqtm = e5a(nc.sbuf_tensor("wqtm", [128, E], BF16))
    wgT = e5a(nc.sbuf_tensor("wgT", [128, 16 * 512], BF16))
    wuT = e5a(nc.sbuf_tensor("wuT", [128, 16 * 512], BF16))
    gb5 = e5a(nc.sbuf_tensor("gb5", [128, 512], F32))
    ub5a = e5a(nc.sbuf_tensor("ub5a", [128, 512], F32))
    hb5 = [e5a(nc.sbuf_tensor(f"hb5{i}", [128, 512], F32)) for i in range(2)]
    hm5 = e5a(nc.sbuf_tensor("hm5", [128, 1], F32))
    cgx = e5a(nc.sbuf_tensor("cgx", [128, TT], F32))
    cux = e5a(nc.sbuf_tensor("cux", [128, TT], F32))
    ps_g = [e5a(nc.psum_tensor(f"ps_g{i}", [128, 512], F32)) for i in range(2)]
    ps_u = [e5a(nc.psum_tensor(f"ps_u{i}", [128, 512], F32)) for i in range(2)]
    ps_t5 = [e5a(nc.psum_tensor(f"ps_t5{i}", [128, 1024], BF16)) for i in range(2)]

    o = kb.op("vector", lambda eng: eng.memset(hmax_all[:], 0.0), "p5hz",
              p2bar)
    ev_hz = kb.inc(o, sem_v)
    o = kb.op("vector", lambda eng: eng.tensor_scalar(
        cgx[:], csx_all[:], wq_sc["g"]["cw"][:], None, ALU.mult), "p5cg",
        [ev_p1_done, ev_sxr_t[-1]])
    ev_cg = kb.inc(o, sem_v)
    o = kb.op("vector", lambda eng: eng.tensor_scalar(
        cux[:], csx_all[:], wq_sc["u"]["cw"][:], None, ALU.mult), "p5cu", [ev_cg])
    ev_cu = kb.inc(o, sem_v)

    ev_xq_t = []
    for t in range(TT):
        _, ev_xq5 = kb.dma("sync", lambda t=t: xqr[t][:], lambda t=t: s_xq[t],
                           f"p5xq{t}",
                           waits=[ev_p2_store[t][2]] + (p2bar if t == 0 else []),
                           ch=f"xq5{t % 2}")
        ev_xq_t.append(ev_xq5)
    ev_xq_all = ev_xq5

    O_CH = [(oc * 512, min(FI, (oc + 1) * 512)) for oc in range(13)]
    prev_sgn5 = None
    prev_wq5 = None
    hst_free = [None, None]
    tp5_free = [None, None]
    gq_free = None
    last_g_mm = None
    ev_h_store = []
    for oc, (o0, o1) in enumerate(O_CH):
        woc = o1 - o0
        nslab = ceil_div(woc, 128)
        ev_wq_done = None
        for wnm, wap, wT, swt in (("g", w_g, wgT, "g"), ("u", w_u, wuT, "u")):
            for s in range(nslab):
                r0 = o0 + s * 128
                nr = min(128, o1 - r0)
                wld = [last_g_mm] if (s == 0 and wnm == "g" and last_g_mm) else []
                if oc == 0 and s == 0 and wnm == "g":
                    wld += [ev_xq_all]
                wld += [prev_sgn5]
                _, ev_w = kb.dma("sync", lambda nr=nr: wslab5[:nr, :],
                                 lambda wap=wap, r0=r0, nr=nr: wap[r0:r0 + nr, :],
                                 f"p5w{wnm}{oc}_{s}", waits=wld, ch=f"w5{wnm}")
                o = kb.op("scalar", lambda eng, nr=nr, swt=swt: eng.activation(
                    absb[:nr, :], wslab5[:nr, :], AF.Abs,
                    scale=wq_sc[swt]["sw"][:nr, :]), f"p5ab{wnm}{oc}_{s}",
                    [ev_w, ev_wsc[(swt, "sw")] if oc == 0 else None,
                     prev_wq5])
                ev_a5 = kb.inc(o, sem_a)
                o = kb.op("scalar", lambda eng, nr=nr: eng.activation(
                    sgnb[:nr, :], wslab5[:nr, :], AF.Sign), f"p5sg{wnm}{oc}_{s}")
                ev_s5 = prev_sgn5 = kb.inc(o, sem_a)
                o = kb.op("vector", lambda eng, nr=nr: eng.tensor_scalar(
                    mskb[:nr, :], absb[:nr, :], 0.5, None, ALU.is_ge),
                    f"p5m{wnm}{oc}_{s}", [ev_a5])
                ev = kb.inc(o, sem_v)
                o = kb.op("vector", lambda eng, nr=nr: eng.tensor_tensor(
                    wqtm[:nr, :], mskb[:nr, :], sgnb[:nr, :], ALU.mult),
                    f"p5q{wnm}{oc}_{s}", [ev, ev_s5])
                ev_q5 = prev_wq5 = kb.inc(o, sem_v)
                for fi in range(16):
                    tpp = fi % 2
                    o = kb.op("tensor", lambda eng, fi=fi, nr=nr, tpp=tpp:
                              eng.transpose(ps_t5[tpp][:, 0:nr],
                                            wqtm[:nr, _ts(fi)], identb[:nr, :nr]),
                              f"p5t{wnm}{oc}_{s}_{fi}",
                              [ev_q5, ev_const["identb"], tp5_free[tpp]])
                    ev_tp = kb.inc(o, sem_t)
                    o = kb.op("vector", lambda eng, fi=fi, s=s, nr=nr, tpp=tpp,
                              wT=wT: eng.tensor_copy(
                                  wT[:, fi * 512 + s * 128:fi * 512 + s * 128 + nr],
                                  ps_t5[tpp][:, 0:nr]),
                              f"p5te{wnm}{oc}_{s}_{fi}", [ev_tp])
                    tp5_free[tpp] = kb.inc(o, sem_v)
                    ev_wq_done = tp5_free[tpp]
        for t in range(TT):
            par = t % 2
            mw = [ev_wq_done, hst_free[par], ev_xq_t[t]]
            o = None
            for fi in range(16):
                o = kb.op("tensor", lambda eng, t=t, fi=fi, par=par, woc=woc:
                          eng.matmul(ps_g[par][:, 0:woc], xqr[t][:, _ts(fi)],
                                     wgT[:, fi * 512:fi * 512 + woc],
                                     start=(fi == 0), stop=(fi == 15)),
                          f"p5mg{oc}_{t}_{fi}", mw if fi == 0 else ())
                o = kb.op("tensor", lambda eng, t=t, fi=fi, par=par, woc=woc:
                          eng.matmul(ps_u[par][:, 0:woc], xqr[t][:, _ts(fi)],
                                     wuT[:, fi * 512:fi * 512 + woc],
                                     start=(fi == 0), stop=(fi == 15)),
                          f"p5mu{oc}_{t}_{fi}", ())
            last_g_mm = ev_mm5 = kb.inc(o, sem_t)
            o = kb.op("scalar", lambda eng, par=par, t=t, woc=woc: eng.activation(
                gb5[:, 0:woc], ps_g[par][:, 0:woc], AF.Silu,
                scale=cgx[:, t:t + 1]), f"p5eg{oc}_{t}",
                [ev_mm5, ev_cg, gq_free])
            ev_g = kb.inc(o, sem_a)
            o = kb.op("scalar", lambda eng, par=par, t=t, woc=woc: eng.activation(
                ub5a[:, 0:woc], ps_u[par][:, 0:woc], AF.Copy,
                scale=cux[:, t:t + 1]), f"p5eu{oc}_{t}", [ev_cu])
            ev_u = kb.inc(o, sem_a)
            o = kb.op("vector", lambda eng, par=par, woc=woc: eng.tensor_tensor(
                hb5[par][:, 0:woc], gb5[:, 0:woc], ub5a[:, 0:woc], ALU.mult),
                f"p5h{oc}_{t}", [ev_g, ev_u,
                                 ev_h_store[-2][1] if len(ev_h_store) >= 2 else None])
            hst_free[par] = ev_h5 = kb.inc(o, sem_v)
            gq_free = ev_h5
            o = kb.op("vector", lambda eng, par=par, woc=woc: eng.reduce_max(
                hm5[:], hb5[par][:, 0:woc], AX.X, apply_absolute_value=True),
                f"p5hm{oc}_{t}", [ev_h5, ev_hz])
            ev = kb.inc(o, sem_v)
            o = kb.op("vector", lambda eng, t=t: eng.tensor_tensor(
                hmax_all[:, t:t + 1], hmax_all[:, t:t + 1], hm5[:], ALU.max),
                f"p5hx{oc}_{t}", [ev])
            kb.inc(o, sem_v)
            _, ev_sh = kb.dma("gpsimd",
                              lambda t=t, o0=o0, o1=o1: s_h[_ts(t), o0:o1],
                              lambda par=par, woc=woc: hb5[par][:, 0:woc],
                              f"p5sh{oc}_{t}", waits=[ev_h5], ch=f"sh{par}")
            ev_h_store.append((f"{oc}_{t}", ev_sh))
    p5a_done = [ev_h_store[-1][1], ev_h_store[-2][1], last_g_mm, ev_wq_done,
                tp5_free[0], tp5_free[1]]
    o = kb.op("vector", lambda eng: eng.tensor_scalar(
        sxh_all[:], hmax_all[:], QEPS, None, ALU.max), "p5bsx",
        p5a_done)
    ev = kb.inc(o, sem_v)
    o = kb.op("vector", lambda eng: eng.reciprocal(csh_all[:], sxh_all[:]),
              "p5brc", [ev])
    ev = kb.inc(o, sem_v)
    o = kb.op("vector", lambda eng: eng.tensor_scalar(
        sxh_all[:], csh_all[:], 127.0, None, ALU.mult), "p5bs2", [ev])
    ev_sxh = kb.inc(o, sem_v)
    o = kb.op("vector", lambda eng: eng.reciprocal(csh_all[:], sxh_all[:]),
              "p5brc2", [ev_sxh])
    ev_csh = kb.inc(o, sem_v)
    o = kb.op("vector", lambda eng: eng.tensor_scalar(
        cdxp[:], csh_all[:], wq_sc["d"]["cw"][:], None, ALU.mult), "p5bcd",
        [ev_csh, ev_wsc[("d", "cw")]])
    ev_cdx = kb.inc(o, sem_v)
    es5a.close()

    if "h" in dbg:
        ap = dbg_out("dbg_h", [T, FI])
        prevd = None
        for t in range(TT):
            _, prevd = kb.dma("gpsimd", lambda a=ap, t=t: a[_ts(t), :],
                              lambda t=t: s_h[_ts(t), :], f"dh{t}",
                              waits=[ev_h_store[-1][1]] + ([prevd] if prevd else []),
                              ch="dbg")


    es3 = ExitStack()
    e3 = es3.enter_context
    xlh = [e3(nc.sbuf_tensor(f"xlh{i}", [128, E], BF16)) for i in range(8)]
    xll = [e3(nc.sbuf_tensor(f"xll{i}", [128, E], BF16)) for i in range(8)]
    wstg = e3(nc.sbuf_tensor("wstg", [128, 8 * 512], F32))
    whib = [e3(nc.sbuf_tensor(f"whib{i}", [128, 16 * 512], BF16)) for i in range(2)]
    wlob = [e3(nc.sbuf_tensor(f"wlob{i}", [128, 16 * 512], BF16)) for i in range(2)]
    qevb = [e3(nc.sbuf_tensor(f"qevb{i}", [128, 512], F32)) for i in range(3)]
    rbuf = e3(nc.sbuf_tensor("rbuf", [128, 512], F32))
    nbuf = e3(nc.sbuf_tensor("nbuf", [128, 512], F32))
    tmpb = e3(nc.sbuf_tensor("tmpb", [128, 256], F32))
    sqw = e3(nc.sbuf_tensor("sqw", [128, 512], F32))
    hst = e3(nc.sbuf_tensor("hst", [128, 32], F32))
    qhib = [e3(nc.sbuf_tensor(f"qhib{i}", [128, 512], BF16)) for i in range(2)]
    qlob = [e3(nc.sbuf_tensor(f"qlob{i}", [128, 512], BF16)) for i in range(2)]
    stg3_hi = e3(nc.sbuf_tensor("stg3_hi", [128, 4 * 1024], BF16))
    stg3_lo = e3(nc.sbuf_tensor("stg3_lo", [128, 4 * 1024], BF16))
    ps_qkv = [e3(nc.psum_tensor(f"ps_qkv{i}", [128, 512], F32)) for i in range(3)]
    ps_tp3 = [e3(nc.psum_tensor(f"ps_tp3{i}", [128, 1024], BF16)) for i in range(2)]

    qk_targets = {0: (s_q_hi, s_q_lo), 1: (s_k_hi, s_k_lo)}
    ev_qkv_store = {"q": {}, "k": {}, "v": {}}

    ci = 0
    ps_free = [None, None, None]
    qb_free = [None, None, None]
    qsplit_free = [None, None]
    pending_tp = []
    chunk_last_ev = None
    chunk_last_ev2 = None

    def flush_tp():
        nonlocal stg_free3, tp_free, qsplit_free, pending_tp
        nonlocal chunk_last_ev, chunk_last_ev2
        for (cj, qp, t8j, foj, ev_hi, ev_split) in pending_tp:
            stg_hi_t, stg_lo_t = stg3_hi, stg3_lo
            last_tp = None
            for src_, stg, pfx, evr in ((qhib[qp], stg_hi_t, "h", ev_hi),
                                        (qlob[qp], stg_lo_t, "l", ev_split)):
                for hh in range(4):
                    tpar = (cj * 8 + (0 if pfx == "h" else 4) + hh) % 2
                    o = kb.op("tensor", lambda eng, src_=src_, hh=hh, tpar=tpar:
                              eng.transpose(ps_tp3[tpar][:, 0:128],
                                            src_[:, hh * 128:(hh + 1) * 128],
                                            identb[:]),
                              f"p3tp{cj}{pfx}{hh}",
                              [evr, ev_const["identb"], tp_free[tpar]])
                    ev_tp = kb.inc(o, sem_t)
                    last_tp = ev_tp
                    dst = lambda stg=stg, hh=hh, t8j=t8j: stg[
                        :, hh * 1024 + t8j * 128:hh * 1024 + t8j * 128 + 128]
                    if hh % 2 == 0:
                        o = kb.op("vector", lambda eng, dst=dst, tpar=tpar:
                                  eng.tensor_copy(dst(), ps_tp3[tpar][:, 0:128]),
                                  f"p3te{cj}{pfx}{hh}",
                                  [ev_tp] + (stg_free3 if (t8j == 0 and stg_free3) else []))
                        tp_free[tpar] = kb.inc(o, sem_v)
                        chunk_last_ev = tp_free[tpar]
                    else:
                        o = kb.op("scalar", lambda eng, dst=dst, tpar=tpar:
                                  eng.activation(dst(), ps_tp3[tpar][:, 0:128],
                                                 AF.Copy),
                                  f"p3te{cj}{pfx}{hh}",
                                  [ev_tp] + (stg_free3 if (t8j == 0 and stg_free3) else []))
                        tp_free[tpar] = kb.inc(o, sem_a)
                        chunk_last_ev2 = tp_free[tpar]
            qsplit_free[qp] = last_tp
        pending_tp = []
    tp_free = [None, None]
    stg_free3 = []
    nbuf_free = None
    sqw_free = None
    lhsT_free = None
    for half in (0, 1):
        ev_lhsT = []
        for t8 in range(8):
            t = half * 8 + t8
            w0 = [ev_p2_store[t][0]] \
                + ((p2bar + p5a_done) if (half == 0 and t8 == 0) else []) \
                + ([lhsT_free] if (half == 1 and t8 == 0) else [])
            _, evh = kb.dma("sync", lambda i=t8: xlh[i][:],
                            lambda t=t: s_xnfm_hi[t], f"p3lh{t}", waits=w0,
                            ch=f"lh{t8 % 2}")
            _, evl = kb.dma("sync", lambda i=t8: xll[i][:],
                            lambda t=t: s_xnfm_lo[t],
                            f"p3ll{t}", waits=[ev_p2_store[t][1]],
                            ch=f"ll{t8 % 2}")
            ev_lhsT += [evh, evl]
        last_mm = None
        last_mm_wpar = [None, None]
        wsplit_ev = [None, None]
        wstg_free = None

        def emit_wload(fo, extra=()):
            nonlocal wsplit_ev, wstg_free
            wpar = fo % 2
            for sub in range(2):
                ev_wl = []
                for g in range(2):
                    gg = sub * 2 + g
                    wget = list(extra) if (sub == 0 and g == 0) else []
                    if wstg_free is not None and g == 0:
                        wget.append(wstg_free)
                    _, evw = kb.dma(
                        "sync",
                        lambda g=g: wstg[:, g * 4 * 512:(g + 1) * 4 * 512],
                        lambda gg=gg, fo=fo: w_qkv[:, fo * 512:(fo + 1) * 512]
                        .rearrange("(fi p) n -> p fi n", p=128)[:, gg * 4:(gg + 1) * 4, :],
                        f"p3w{half}_{fo}_{gg}", waits=wget, ch=f"w3{gg}")
                    ev_wl.append(evw)
                wl = [last_mm_wpar[wpar]] if (sub == 0 and last_mm_wpar[wpar]) else []
                o = kb.op("vector", lambda eng, sub=sub, wpar=wpar: eng.tensor_copy(
                    whib[wpar][:, sub * 4096:(sub + 1) * 4096], wstg[:]),
                    f"p3wh{half}_{fo}_{sub}", ev_wl + wl)
                ev = kb.inc(o, sem_v)
                o = kb.op("vector", lambda eng, sub=sub, wpar=wpar: eng.tensor_tensor(
                    wlob[wpar][:, sub * 4096:(sub + 1) * 4096], wstg[:],
                    whib[wpar][:, sub * 4096:(sub + 1) * 4096], ALU.subtract),
                    f"p3wl{half}_{fo}_{sub}", [ev])
                wstg_free = kb.inc(o, sem_v)
            wsplit_ev[wpar] = wstg_free

        emit_wload(0, extra=((p2bar + p5a_done) if half == 0 else [lhsT_free]))
        for fo in range(_P3_FO_LIMIT):
            wpar = fo % 2
            if fo + 1 < _P3_FO_LIMIT:
                emit_wload(fo + 1)
            wsplit_done = wsplit_ev[wpar]
            for t8 in range(8):
                t = half * 8 + t8
                par = ci % 3
                mm_waits = [wsplit_done, ps_free[par]] + ev_lhsT
                o = None
                for fi in range(16):
                    for (lt, rt) in ((xlh[t8], whib[wpar]), (xlh[t8], wlob[wpar]),
                                     (xll[t8], whib[wpar])):
                        first = (fi == 0 and lt is xlh[t8] and rt is whib[wpar])
                        last = (fi == 15 and lt is xll[t8])
                        o = kb.op("tensor", lambda eng, lt=lt, rt=rt, fi=fi, par=par,
                                  first=first, last=last: eng.matmul(
                                      ps_qkv[par][:], lt[:, _ts(fi)],
                                      rt[:, fi * 512:(fi + 1) * 512],
                                      start=first, stop=last),
                                  f"p3mm{half}_{fo}_{t8}_{fi}",
                                  mm_waits if first else ())
                ev_mm = kb.inc(o, sem_t)
                last_mm = ev_mm
                last_mm_wpar[wpar] = ev_mm
                flush_tp()
                if fo < 8:
                    kind = 0 if fo < 4 else 1
                    h0 = (fo % 4) * 4
                    o = kb.op("scalar", lambda eng, par=par: eng.activation(
                        qevb[par][:], ps_qkv[par][:], AF.Copy),
                        f"p3ev{half}_{fo}_{t8}", [ev_mm, qb_free[par]])
                    ev_qb = kb.inc(o, sem_a)
                    ps_free[par] = ev_qb
                    H4 = [128, 4]
                    J64 = [1, 64]
                    x1 = _ap3(qevb[par], 0, H4, J64)
                    x2 = _ap3(qevb[par], 64, H4, J64)
                    r1 = _ap3(rbuf, 0, H4, J64)
                    r2 = _ap3(rbuf, 64, H4, J64)
                    t4 = _ap3(tmpb, 0, [64, 4], J64)
                    cosT = _ap3(cosb, t * 64, [0, 4], J64)
                    sinT = _ap3(sinb, t * 64, [0, 4], J64)
                    o = kb.op("vector", lambda eng, r1=r1, x1=x1, cosT=cosT:
                              eng.tensor_tensor(r1, x1, cosT, ALU.mult),
                              f"p3r1{ci}", [ev_qb, ev_const["cos"],
                                            ev_const["sin"], nbuf_free])
                    ev = kb.inc(o, sem_v)
                    o = kb.op("vector", lambda eng, t4=t4, x2=x2, sinT=sinT:
                              eng.tensor_tensor(t4, x2, sinT, ALU.mult),
                              f"p3r2{ci}", [ev])
                    ev = kb.inc(o, sem_v)
                    o = kb.op("vector", lambda eng, r1=r1, t4=t4:
                              eng.tensor_tensor(r1, r1, t4, ALU.subtract),
                              f"p3r3{ci}", [ev])
                    ev = kb.inc(o, sem_v)
                    o = kb.op("vector", lambda eng, r2=r2, x1=x1, sinT=sinT:
                              eng.tensor_tensor(r2, x1, sinT, ALU.mult),
                              f"p3r4{ci}", [ev])
                    ev = kb.inc(o, sem_v)
                    o = kb.op("vector", lambda eng, t4=t4, x2=x2, cosT=cosT:
                              eng.tensor_tensor(t4, x2, cosT, ALU.mult),
                              f"p3r5{ci}", [ev])
                    ev = kb.inc(o, sem_v)
                    o = kb.op("vector", lambda eng, r2=r2, t4=t4:
                              eng.tensor_tensor(r2, r2, t4, ALU.add),
                              f"p3r6{ci}", [ev])
                    last_rope = kb.inc(o, sem_v)
                    qb_free[par] = last_rope
                    r3 = _ap3(rbuf, 0, H4, [1, 128])
                    n3 = _ap3(nbuf, 0, H4, [1, 128])
                    o = kb.op("vector", lambda eng, r3=r3: eng.reduce_sum(
                        hst[:, 0:4], r3, AX.X), f"p3l1{ci}", [last_rope])
                    ev = kb.inc(o, sem_v)
                    o = kb.op("vector", lambda eng: eng.tensor_scalar(
                        hst[:, 4:8], hst[:, 0:4], 1.0 / 128, None, ALU.mult),
                        f"p3l2{ci}", [ev])
                    ev_mean = kb.inc(o, sem_v)
                    o = kb.op("vector", lambda eng: eng.tensor_tensor(
                        sqw[:], rbuf[:], rbuf[:], ALU.mult), f"p3l3{ci}",
                        [last_rope])
                    ev = kb.inc(o, sem_v)
                    o = kb.op("vector", lambda eng: eng.reduce_sum(
                        hst[:, 8:12], _ap3(sqw, 0, H4, [1, 128]), AX.X),
                        f"p3l4{ci}", [ev])
                    ev = kb.inc(o, sem_v)
                    o = kb.op("vector", lambda eng: eng.tensor_tensor(
                        hst[:, 12:16], hst[:, 4:8], hst[:, 4:8], ALU.mult),
                        f"p3l5{ci}", [ev_mean])
                    ev = kb.inc(o, sem_v)
                    o = kb.op("vector", lambda eng: eng.scalar_tensor_tensor(
                        hst[:, 16:20], hst[:, 8:12], 1.0 / 128, hst[:, 12:16],
                        ALU.mult, ALU.subtract), f"p3l6{ci}", [ev])
                    ev = kb.inc(o, sem_v)
                    o = kb.op("vector", lambda eng: eng.tensor_scalar(
                        hst[:, 20:24], hst[:, 16:20], LN_EPS, None, ALU.add),
                        f"p3l7{ci}", [ev])
                    ev = kb.inc(o, sem_v)
                    o = kb.op("vector", lambda eng: eng.reciprocal(
                        hst[:, 24:28], hst[:, 20:24]), f"p3l8{ci}", [ev])
                    ev = kb.inc(o, sem_v)
                    o = kb.op("scalar", lambda eng: eng.activation(
                        hst[:, 28:32], hst[:, 24:28], AF.Sqrt), f"p3l9{ci}", [ev])
                    ev_rs = kb.inc(o, sem_a)
                    mean_b = _ap3(hst, 4, [1, 4], [0, 128])
                    rstd_b = _ap3(hst, 28, [1, 4], [0, 128])
                    lnsc_b = _ap3(lnscb, 0, [0, 4], [1, 128])
                    lnbi_b = _ap3(lnbib, 0, [0, 4], [1, 128])
                    o = kb.op("vector", lambda eng, n3=n3, r3=r3, mean_b=mean_b:
                              eng.tensor_tensor(n3, r3, mean_b, ALU.subtract),
                              f"p3lA{ci}", [ev_rs])
                    ev = kb.inc(o, sem_v)
                    o = kb.op("vector", lambda eng, n3=n3, rstd_b=rstd_b:
                              eng.tensor_tensor(n3, n3, rstd_b, ALU.mult),
                              f"p3lB{ci}", [ev])
                    ev = kb.inc(o, sem_v)
                    o = kb.op("vector", lambda eng, n3=n3, lnsc_b=lnsc_b:
                              eng.tensor_tensor(n3, n3, lnsc_b, ALU.mult),
                              f"p3lC{ci}", [ev, ev_const["lnsc"], ev_const["lnbi"]])
                    ev = kb.inc(o, sem_v)
                    o = kb.op("vector", lambda eng, n3=n3, lnbi_b=lnbi_b:
                              eng.tensor_tensor(n3, n3, lnbi_b, ALU.add),
                              f"p3lD{ci}", [ev])
                    ev_lnd = kb.inc(o, sem_v)
                    qp = ci % 2
                    o = kb.op("scalar", lambda eng, qp=qp: eng.activation(
                        qhib[qp][:], nbuf[:], AF.Copy), f"p3hi{ci}",
                        [ev_lnd, qsplit_free[qp]])
                    ev_hi = kb.inc(o, sem_a)
                    o = kb.op("vector", lambda eng, qp=qp: eng.tensor_tensor(
                        qlob[qp][:], nbuf[:], qhib[qp][:], ALU.subtract),
                        f"p3lo{ci}", [ev_hi])
                    ev_split = kb.inc(o, sem_v)
                    nbuf_free = ev_split
                    pending_tp.append((ci, qp, t8, fo, ev_hi, ev_split))
                    if "qkv" in dbg or "qk" in dbg:
                        nmq = "dbg_q" if kind == 0 else "dbg_k"
                        ap = dbg_aps.get(nmq) or dbg_out(nmq, [T, E])
                        o, evd = kb.dma("gpsimd",
                                        lambda a=ap, t=t, fo=fo: a[_ts(t), (fo % 4) * 512:(fo % 4) * 512 + 512],
                                        lambda: nbuf[:], f"d3{ci}", waits=[ev_split],
                                        ch="dbg")
                        nbuf_free = evd
                else:
                    h0 = (fo - 8) * 4
                    evs_c = []
                    for hh in range(4):
                        dst = lambda stg, hh=hh, t8=t8: stg[:, hh * 1024 + t8 * 128:
                                                           hh * 1024 + t8 * 128 + 128]
                        o = kb.op("vector", lambda eng, par=par, hh=hh, t8=t8:
                                  eng.tensor_copy(
                                      stg3_hi[:, hh * 1024 + t8 * 128:hh * 1024 + t8 * 128 + 128],
                                      ps_qkv[par][:, hh * 128:(hh + 1) * 128]),
                                  f"p3vh{ci}_{hh}",
                                  [ev_mm] + (stg_free3 if (t8 == 0 and hh == 0 and stg_free3) else []))
                        ev = kb.inc(o, sem_v)
                        o = kb.op("vector", lambda eng, par=par, hh=hh, t8=t8:
                                  eng.tensor_tensor(
                                      stg3_lo[:, hh * 1024 + t8 * 128:hh * 1024 + t8 * 128 + 128],
                                      ps_qkv[par][:, hh * 128:(hh + 1) * 128],
                                      stg3_hi[:, hh * 1024 + t8 * 128:hh * 1024 + t8 * 128 + 128],
                                      ALU.subtract), f"p3vl{ci}_{hh}", [ev])
                        evs_c.append(kb.inc(o, sem_v))
                    ps_free[par] = evs_c[-1]
                    chunk_last_ev = evs_c[-1]
                    chunk_last_ev2 = None
                    if "qkv" in dbg or "vv" in dbg:
                        ap = dbg_aps.get("dbg_v") or dbg_out("dbg_v", [T, E])
                        o = kb.op("scalar", lambda eng, par=par: eng.activation(
                            qevb[par][:], ps_qkv[par][:], AF.Copy),
                            f"p3dve{ci}", [ev_mm, evs_c[-1], qb_free[par]])
                        ev_dv = kb.inc(o, sem_a)
                        ps_free[par] = ev_dv
                        o, evd = kb.dma("gpsimd",
                                        lambda a=ap, t=t, fo=fo: a[_ts(t), (fo - 8) * 512:(fo - 8) * 512 + 512],
                                        lambda par=par: qevb[par][:], f"d3v{ci}",
                                        waits=[ev_dv], ch="dbg")
                        qb_free[par] = evd
                ci += 1
            flush_tp()
            if fo < 4:
                tgt_hi, tgt_lo = s_q_hi, s_q_lo
                hb = fo * 4
                key = "q"
            elif fo < 8:
                tgt_hi, tgt_lo = s_k_hi, s_k_lo
                hb = (fo - 4) * 4
                key = "k"
            else:
                tgt_hi, tgt_lo = s_v_hi, s_v_lo
                hb = (fo - 8) * 4
                key = "v"
            st_evs = []
            for hh in range(4):
                for stg, tgt in ((stg3_hi, tgt_hi), (stg3_lo, tgt_lo)):
                    _, ev_s = kb.dma(
                        "gpsimd",
                        lambda tgt=tgt, hb=hb, hh=hh, half=half:
                            tgt[hb + hh, :, half * 1024:(half + 1) * 1024],
                        lambda stg=stg, hh=hh: stg[:, hh * 1024:(hh + 1) * 1024],
                        f"p3st{half}_{fo}_{hh}",
                        waits=[chunk_last_ev, chunk_last_ev2],
                        ch=f"st3{hh}")
                    st_evs.append(ev_s)
            stg_free3 = [st_evs[i * 2 + 1] for i in range(4)]
            for hh in range(4):
                ev_qkv_store[key][(hb + hh, half)] = st_evs[hh * 2 + 1]
        lhsT_free = last_mm
    p3_done = stg_free3 + [last_mm, wsplit_done]
    es3.close()

    es4 = ExitStack()
    e4 = es4.enter_context
    qs_hi = [e4(nc.sbuf_tensor(f"qs_hi{i}", [128, T], BF16)) for i in range(2)]
    qs_lo = [e4(nc.sbuf_tensor(f"qs_lo{i}", [128, T], BF16)) for i in range(2)]
    ks_hi = [e4(nc.sbuf_tensor(f"ks_hi{i}", [128, T], BF16)) for i in range(2)]
    ks_lo = [e4(nc.sbuf_tensor(f"ks_lo{i}", [128, T], BF16)) for i in range(2)]
    vs_hi = [e4(nc.sbuf_tensor(f"vs_hi{i}", [128, T], BF16)) for i in range(2)]
    vs_lo = [e4(nc.sbuf_tensor(f"vs_lo{i}", [128, T], BF16)) for i in range(2)]
    s2b = [e4(nc.sbuf_tensor(f"s2b{i}", [128, 512], F32)) for i in range(2)]
    afb = [e4(nc.sbuf_tensor(f"afb{i}", [128, 512], F32)) for i in range(2)]
    ahib = [e4(nc.sbuf_tensor(f"ahib{i}", [128, 512], BF16)) for i in range(2)]
    alob = [e4(nc.sbuf_tensor(f"alob{i}", [128, 512], BF16)) for i in range(2)]
    rdv = e4(nc.sbuf_tensor("rdv", [1, 512], F32))
    rdbf = e4(nc.sbuf_tensor("rdbf", [128, 512], F32))
    tfb = e4(nc.sbuf_tensor("tfb", [128, 512], F32))
    at_hi_stg = e4(nc.sbuf_tensor("at_hi_stg", [128, T], BF16))
    at_lo_stg = e4(nc.sbuf_tensor("at_lo_stg", [128, T], BF16))
    ps_s = [e4(nc.psum_tensor(f"ps_s{i}", [128, 512], F32)) for i in range(3)]
    ps_o = [e4(nc.psum_tensor(f"ps_o{i}", [128, 512], F32)) for i in range(2)]
    ps_d = e4(nc.psum_tensor("ps_d", [128, 512], F32))
    ps_b = e4(nc.psum_tensor("ps_b", [128, 512], F32))

    CHUNKS = []
    for c in range(4):
        kmax = min(TT - 1, 4 * c + 3)
        for kb_ in range(kmax + 1):
            q0 = max(512 * c, 128 * kb_)
            CHUNKS.append((c, kb_, q0, 512 * (c + 1)))

    strip_free = [None, None]
    sq_free = [None, None, None]
    s2_free = [None, None]
    af_free = [None, None]
    a_free = [None, None]
    po_free = [None, None]
    pd_free = None
    pb_free = None
    tf_free = None
    stg4_free = None
    dbg_at_free = None
    ev_p4_store = []
    if "attn" in dbg:
        dbg_out("dbg_attn", [E, T])

    for h in range(H):
        hb = h % 2
        ld_evs = []
        ldw = [p3_done if h == 0 else [strip_free[hb]]][0]
        for nm2, bufs, srcs in (("qh", qs_hi, s_q_hi), ("ql", qs_lo, s_q_lo),
                                ("kh", ks_hi, s_k_hi), ("kl", ks_lo, s_k_lo),
                                ("vh", vs_hi, s_v_hi), ("vl", vs_lo, s_v_lo)):
            w4 = list(ldw)
            if h < 2:
                kk = nm2[0]
                w4 += [ev_qkv_store[kk][(h, 0)], ev_qkv_store[kk][(h, 1)]]
            _, ev_l = kb.dma("sync", lambda b=bufs, hb=hb: b[hb][:],
                             lambda s=srcs, h=h: s[h], f"p4l{h}{nm2}",
                             waits=w4, ch="p4" + nm2)
            ld_evs.append(ev_l)

        ev_alo_c = {}
        ev_av_last = None
        ev_dmm = None

        def emit_S(idx):
            c, kb_, q0, q1 = CHUNKS[idx]
            w = q1 - q0
            sp = idx % 3
            par = idx % 2
            mmw = ld_evs + [sq_free[sp]]
            o = None
            for ic, (lt, rt) in enumerate(((ks_hi[hb], qs_hi[hb]),
                                           (ks_hi[hb], qs_lo[hb]),
                                           (ks_lo[hb], qs_hi[hb]))):
                o = kb.op("tensor", lambda eng, lt=lt, rt=rt, kb_=kb_, q0=q0,
                          q1=q1, sp=sp, ic=ic: eng.matmul(
                              ps_s[sp][:, 0:q1 - q0], lt[:, _ts(kb_)],
                              rt[:, q0:q1], start=(ic == 0), stop=(ic == 2)),
                          f"p4s{h}_{idx}_{ic}", mmw if ic == 0 else ())
            ev_mm = kb.inc(o, sem_t)
            o = kb.op("scalar", lambda eng, w=w, sp=sp, par=par: eng.activation(
                s2b[par][:, 0:w], ps_s[sp][:, 0:w], AF.Square),
                f"p4sq{h}_{idx}", [ev_mm, s2_free[par]])
            ev_sq = kb.inc(o, sem_a)
            sq_free[sp] = ev_sq
            o = kb.op("vector", lambda eng, w=w, par=par: eng.tensor_tensor(
                afb[par][:, 0:w], s2b[par][:, 0:w], s2b[par][:, 0:w], ALU.mult),
                f"p4a4{h}_{idx}", [ev_sq, af_free[par]])
            ev = kb.inc(o, sem_v)
            s2_free[par] = ev
            if q0 == kb_ * 128:
                o = kb.op("vector", lambda eng, par=par: eng.tensor_tensor(
                    afb[par][:, 0:128], afb[par][:, 0:128], trim[:], ALU.mult),
                    f"p4msk{h}_{idx}", [ev, ev_const["trim"]])
                ev = kb.inc(o, sem_v)
            o = kb.op("scalar", lambda eng, w=w, par=par: eng.activation(
                ahib[par][:, 0:w], afb[par][:, 0:w], AF.Copy),
                f"p4ah{h}_{idx}", [ev, a_free[par],
                                   stg4_free if idx < 2 and h > 0 else None])
            ev_hi = kb.inc(o, sem_a)
            af_free[par] = ev_hi
            o = kb.op("vector", lambda eng, w=w, par=par: eng.tensor_tensor(
                alob[par][:, 0:w], afb[par][:, 0:w], ahib[par][:, 0:w],
                ALU.subtract), f"p4al{h}_{idx}", [ev_hi])
            ev_alo_c[idx] = kb.inc(o, sem_v)

        def emit_av(idx):
            nonlocal ev_av_last, ev_dmm
            c, kb_, q0, q1 = CHUNKS[idx]
            w = q1 - q0
            par = idx % 2
            kmax = min(TT - 1, 4 * c + 3)
            off = q0 - 512 * c
            wv = [ev_alo_c.pop(idx)]
            if kb_ == 0:
                wv.append(po_free[c % 2])
            o = None
            for ic, (lt, rt) in enumerate(((vs_hi[hb], ahib[par]),
                                           (vs_hi[hb], alob[par]),
                                           (vs_lo[hb], ahib[par]))):
                o = kb.op("tensor", lambda eng, lt=lt, rt=rt, kb_=kb_, off=off,
                          w=w, ic=ic, c=c, kmax=kmax: eng.matmul(
                              ps_o[c % 2][:, off:off + w], lt[:, _ts(kb_)],
                              rt[:, 0:w], start=(kb_ == 0 and ic == 0),
                              stop=(kb_ == kmax and ic == 2)),
                          f"p4o{h}_{idx}_{ic}", wv if ic == 0 else ())
            ev_av_last = kb.inc(o, sem_t)
            dw = []
            if kb_ == 0:
                dw.append(pd_free)
            for ic, rt in enumerate((ahib[par], alob[par])):
                o = kb.op("tensor", lambda eng, rt=rt, off=off, w=w, ic=ic,
                          kb_=kb_, kmax=kmax: eng.matmul(
                              ps_d[0:1, off:off + w], ones_k[:], rt[:, 0:w],
                              start=(kb_ == 0 and ic == 0),
                              stop=(kb_ == kmax and ic == 1)),
                          f"p4d{h}_{idx}_{ic}",
                          dw if ic == 0 else ())
            ev_dmm = kb.inc(o, sem_t)
            a_free[par] = ev_dmm

        def emit_tail(c):
            nonlocal pd_free, pb_free, tf_free, dbg_at_free, stg4_free
            nonlocal last_stgV, last_stgA
            o = kb.op("vector", lambda eng: eng.reciprocal(
                rdv[:], ps_d[0:1, :]), f"p4r{h}_{c}", [ev_dmm])
            pd_free = ev_recip = kb.inc(o, sem_v)
            o = kb.op("tensor", lambda eng: eng.matmul(
                ps_b[:, 0:512], ones_row[0:1, :], rdv[0:1, :],
                start=True, stop=True), f"p4b{h}_{c}",
                [ev_recip, ev_const["ones_row"], pb_free])
            ev_b = kb.inc(o, sem_t)
            o = kb.op("vector", lambda eng: eng.tensor_copy(
                rdbf[:], ps_b[:, 0:512]), f"p4be{h}_{c}", [ev_b])
            pb_free = kb.inc(o, sem_v)
            o = kb.op("vector", lambda eng, c=c: eng.tensor_tensor(
                tfb[:], ps_o[c % 2][:, :], rdbf[:], ALU.mult),
                f"p4n{h}_{c}", [ev_av_last, pb_free, tf_free, dbg_at_free])
            ev_tf = kb.inc(o, sem_v)
            po_free[c % 2] = ev_tf
            o = kb.op("scalar", lambda eng, c=c: eng.activation(
                at_hi_stg[:, _ts(c, 512)], tfb[:], AF.Copy), f"p4nh{h}_{c}",
                [ev_tf, stg4_free if c == 0 else None])
            tf_free = ev_hi2 = kb.inc(o, sem_a)
            last_stgA = ev_hi2
            o = kb.op("vector", lambda eng, c=c: eng.tensor_tensor(
                at_lo_stg[:, _ts(c, 512)], tfb[:], at_hi_stg[:, _ts(c, 512)],
                ALU.subtract), f"p4nl{h}_{c}", [ev_hi2])
            last_stgV = kb.inc(o, sem_v)
            if "attn" in dbg:
                ap = dbg_aps["dbg_attn"]
                _, dbg_at_free = kb.dma(
                    "gpsimd", lambda a=ap, h=h, c=c: a[_ts(h), _ts(c, 512)],
                    lambda: tfb[:], f"d4{h}_{c}", waits=[ev_tf], ch="dbg")

        last_stgV = None
        last_stgA = None
        emit_S(0)
        if len(CHUNKS) > 1:
            emit_S(1)
        for idx in range(len(CHUNKS)):
            if idx + 2 < len(CHUNKS):
                emit_S(idx + 2)
            emit_av(idx)
            c, kb_, q0, q1 = CHUNKS[idx]
            if idx + 1 == len(CHUNKS) or CHUNKS[idx + 1][0] != c:
                emit_tail(c)

        st4 = []
        for stg, tgt, chn in ((at_hi_stg, s_at_hi, "p4sth"),
                              (at_lo_stg, s_at_lo, "p4stl")):
            _, ev_s = kb.dma(
                "gpsimd",
                lambda tgt=tgt, h=h: tgt[:, :, _ts(h)].transpose([1, 0, 2]),
                lambda stg=stg: stg[:], f"p4st{h}",
                waits=[last_stgV, last_stgA], ch=chn)
            st4.append(ev_s)
        stg4_free = st4[-1]
        ev_p4_store.append(st4)
        strip_free[hb] = ev_av_last

    es4.close()
    p4_done = [stg4_free, ev_av_last, pb_free]

    es4b = ExitStack()
    e4b = es4b.enter_context
    wos = e4b(nc.sbuf_tensor("wos", [128, 1024], F32))
    wo_hi = e4b(nc.sbuf_tensor("wo_hi", [128, 16 * 1024], BF16))
    wo_lo = e4b(nc.sbuf_tensor("wo_lo", [128, 16 * 1024], BF16))
    at4h = [e4b(nc.sbuf_tensor(f"at4h{i}", [128, E], BF16)) for i in range(2)]
    at4l = [e4b(nc.sbuf_tensor(f"at4l{i}", [128, E], BF16)) for i in range(2)]
    yab = [e4b(nc.sbuf_tensor(f"yab{i}", [128, 1024], F32)) for i in range(2)]
    ps_y = [e4b(nc.psum_tensor(f"ps_y{i}", [128, 1024], F32)) for i in range(3)]
    q_hcb = e4b(nc.sbuf_tensor("q_hcb", [128, 2048], F32))
    q_z = e4b(nc.sbuf_tensor("q_z", [128, 2048], F32))
    q_u = e4b(nc.sbuf_tensor("q_u", [128, 2048], F32))
    q_c = e4b(nc.sbuf_tensor("q_c", [128, 2048], F32))
    q_hqtm = e4b(nc.sbuf_tensor("q_hqtm", [128, FI], BF16))
    q_stg = e4b(nc.sbuf_tensor("q_stg", [128, NIC * 128], BF16))
    ps_th = [e4b(nc.psum_tensor(f"ps_th{i}", [128, 1024], BF16)) for i in range(2)]

    ev_hq_store = [None] * TT
    hqs = {"hcb_free": None, "nop": None, "tp_free": [None, None],
           "stg_free": None, "stg_free_hi": None, "hqtm_free": None,
           "zero": None, "peel": []}

    def _hq_zero(t=None):
        o = kb.op("vector", lambda eng: eng.memset(
            q_stg[64:128, (NIC - 1) * 128:NIC * 128], 0.0), "hqz", p4_done)
        hqs["zero"] = kb.inc(o, sem_v)

    def _hq_load(t, cc):
        c0 = cc * 2048
        c1 = min(FI, c0 + 2048)
        w = ([ev_h_store[-1][1], ev_h_store[-2][1]] if (t == 0 and cc == 0) else []) \
            + ([hqs["hcb_free"]] if hqs["hcb_free"] else [])
        _, ev = kb.dma("sync", lambda c0=c0, c1=c1: q_hcb[:, 0:c1 - c0],
                       lambda t=t, c0=c0, c1=c1: s_h[_ts(t), c0:c1],
                       f"hql{t}_{cc}", waits=w, ch="lh5")
        hqs["load"] = ev

    def _hq_z(t, cc):
        c0 = cc * 2048
        c1 = min(FI, c0 + 2048)
        o = kb.op("scalar", lambda eng, t=t, c0=c0, c1=c1: eng.activation(
            q_z[:, 0:c1 - c0], q_hcb[:, 0:c1 - c0], AF.Copy, bias=128.5,
            scale=sxh_all[:, t:t + 1]), f"hqz{t}_{cc}",
            [hqs["load"], ev_sxh, hqs["nop"]])
        hqs["zev"] = kb.inc(o, sem_a)
        hqs["hcb_free"] = hqs["zev"]

    def _hq_peel(t, cc):
        c0 = cc * 2048
        c1 = min(FI, c0 + 2048)
        ew = [hqs["hqtm_free"]] if (cc == 0 and hqs["hqtm_free"]) else []
        done = emit_peel(
            kb, (sem_v, sem_a, sem_g), q_z, q_u, q_c,
            lambda o0, w2, c0=c0: q_hqtm[:, c0 + o0:c0 + o0 + w2],
            0, c1 - c0, hqs["zev"], f"hqp{t}_{cc}_", extra_waits=ew,
            bias_m128=(biasm128, ev_const["bm128"]))
        o = kb.op("vector", lambda eng: eng.engine_nop(), f"hqn{t}_{cc}", done)
        hqs["nop"] = kb.inc(o, sem_v)
        hqs["peel"].append(hqs["nop"])

    def _hq_tp(t, icg):
        last = None
        for ic in range(icg * 4, min(NIC, icg * 4 + 4)):
            wic = min(128, FI - ic * 128)
            tpp = ic % 2
            stw = hqs["stg_free"] if ic == 0 else \
                hqs["stg_free_hi"] if ic == 24 else None
            o = kb.op("tensor", lambda eng, ic=ic, wic=wic, tpp=tpp:
                      eng.transpose(ps_th[tpp][:wic, 0:128],
                                    q_hqtm[:, ic * 128:ic * 128 + wic],
                                    identb[:]),
                      f"hqt{t}_{ic}",
                      hqs["peel"] + [hqs["tp_free"][tpp]] +
                      ([stw] if stw else []))
            ev_tp = kb.inc(o, sem_t)
            last = ev_tp
            o = kb.op("vector", lambda eng, ic=ic, wic=wic, tpp=tpp:
                      eng.tensor_copy(q_stg[:wic, ic * 128:(ic + 1) * 128],
                                      ps_th[tpp][:wic, 0:128]),
                      f"hqe{t}_{ic}", [ev_tp])
            hqs["tp_free"][tpp] = kb.inc(o, sem_v)
        if icg == 12:
            hqs["hqtm_free"] = last
            hqs["peel"] = []

    def _hq_store_lo(t):
        _, ev = kb.dma("gpsimd", lambda t=t: s_hq[t, :, 0:24 * 128],
                       lambda: q_stg[:, 0:24 * 128], f"hqsl{t}",
                       waits=[hqs["tp_free"][0], hqs["tp_free"][1]],
                       ch="shq")
        hqs["stg_free"] = ev

    def _hq_store_hi(t):
        _, ev = kb.dma("gpsimd", lambda t=t: s_hq[t, :, 24 * 128:NIC * 128],
                       lambda: q_stg[:, 24 * 128:NIC * 128], f"hqsh{t}",
                       waits=[hqs["tp_free"][0], hqs["tp_free"][1], hqs["zero"]],
                       ch="shq")
        ev_hq_store[t] = ev
        hqs["stg_free_hi"] = ev

    hq_steps = [lambda t=None: _hq_zero()]
    for t in range(TT):
        for cc in range(4):
            hq_steps.append(lambda t=t, cc=cc: _hq_load(t, cc))
            hq_steps.append(lambda t=t, cc=cc: _hq_z(t, cc))
            hq_steps.append(lambda t=t, cc=cc: _hq_peel(t, cc))
        for icg in range(13):
            hq_steps.append(lambda t=t, icg=icg: _hq_tp(t, icg))
            if icg == 5:
                hq_steps.append(lambda t=t: _hq_store_lo(t))
        hq_steps.append(lambda t=t: _hq_store_hi(t))
    hq_state_i = [0]

    def hq_pump(n):
        for _ in range(n):
            if hq_state_i[0] >= len(hq_steps):
                return
            hq_steps[hq_state_i[0]]()
            hq_state_i[0] += 1

    at_free = [None, None]
    ya_free = [None, None]
    psy_free = [None, None, None]
    ev_ya_store = []
    wsplit4_done = None
    for foh in (0, 1):
        last_wsp = None
        for Kc in range(16):
            w4b = p4_done if (foh == 0 and Kc == 0) else \
                ([wsplit4_done] if Kc == 0 else [])
            _, ev_w = kb.dma("sync", lambda Kc=Kc, foh=foh:
                             wos[:], lambda Kc=Kc, foh=foh:
                             w_out[_ts(Kc), foh * 1024:(foh + 1) * 1024],
                             f"p4bw{foh}_{Kc}", waits=w4b + [last_wsp] if last_wsp else w4b,
                             ch="wo5")
            o = kb.op("vector", lambda eng, Kc=Kc: eng.tensor_copy(
                wo_hi[:, Kc * 1024:(Kc + 1) * 1024], wos[:]),
                f"p4bh{foh}_{Kc}", [ev_w])
            ev = kb.inc(o, sem_v)
            o = kb.op("vector", lambda eng, Kc=Kc: eng.tensor_tensor(
                wo_lo[:, Kc * 1024:(Kc + 1) * 1024], wos[:],
                wo_hi[:, Kc * 1024:(Kc + 1) * 1024], ALU.subtract),
                f"p4bl{foh}_{Kc}", [ev])
            last_wsp = kb.inc(o, sem_v)
        wsplit4_done = last_wsp
        for t in range(TT):
            par = t % 3
            apar = t % 2
            wld = [ev_p4_store[-1][0], ev_p4_store[-1][1]] if (foh == 0 and t < 2) else []
            _, evh = kb.dma("sync", lambda par=apar: at4h[par][:],
                            lambda t=t: s_at_hi[t], f"p4bah{foh}_{t}",
                            waits=wld + ([at_free[apar]] if at_free[apar] else []),
                            ch="at4h")
            _, evl = kb.dma("sync", lambda par=apar: at4l[par][:],
                            lambda t=t: s_at_lo[t], f"p4bal{foh}_{t}",
                            waits=wld + ([at_free[apar]] if at_free[apar] else []),
                            ch="at4l")
            ev_mm4 = None
            mw = [evh, evl, wsplit4_done, psy_free[par]]
            for Kc in range(16):
                for ic, (lt, rt) in enumerate(((at4h[apar], wo_hi),
                                               (at4h[apar], wo_lo),
                                               (at4l[apar], wo_hi))):
                    for half2 in range(2):
                        o = kb.op("tensor", lambda eng, lt=lt, rt=rt, Kc=Kc,
                                  par=par, h2=half2, ic=ic: eng.matmul(
                                      ps_y[par][:, h2 * 512:(h2 + 1) * 512],
                                      lt[:, _ts(Kc)],
                                      rt[:, Kc * 1024 + h2 * 512:Kc * 1024 + (h2 + 1) * 512],
                                      start=(Kc == 0 and ic == 0),
                                      stop=(Kc == 15 and ic == 2)),
                                  f"p4bm{foh}_{t}_{Kc}_{ic}_{half2}",
                                  mw if (Kc == 0 and ic == 0 and half2 == 0) else ())
                        if Kc == 15 and ic == 2 and half2 == 1:
                            ev_mm4 = kb.inc(o, sem_t)
                if Kc % 2 == 1 and not _HQ_NOPUMP:
                    hq_pump(1)
            at_free[apar] = ev_mm4
            o = kb.op("vector", lambda eng, par=par, apar=apar: eng.tensor_copy(
                yab[apar][:], ps_y[par][:]), f"p4bev{foh}_{t}",
                [ev_mm4] + ([ya_free[apar]] if ya_free[apar] else []))
            psy_free[par] = ev_ev4 = kb.inc(o, sem_v)
            _, ev_st = kb.dma("gpsimd",
                              lambda t=t, foh=foh: s_ya[_ts(t), foh * 1024:(foh + 1) * 1024],
                              lambda par=apar: yab[par][:], f"p4bst{foh}_{t}",
                              waits=[ev_ev4], ch="ya")
            ya_free[apar] = ev_st
            ev_ya_store.append(ev_st)
    hq_pump(len(hq_steps))
    p4b_done = [ev_ya_store[-1], psy_free[0], psy_free[1], psy_free[2],
                at_free[0], at_free[1], ev_hq_store[-1],
                hqs["tp_free"][0], hqs["tp_free"][1], hqs["nop"]]
    es4b.close()

    if "ya" in dbg:
        ap = dbg_out("dbg_ya", [T, E])
        prevd = None
        for t in range(TT):
            _, prevd = kb.dma("gpsimd", lambda a=ap, t=t: a[_ts(t), :],
                              lambda t=t: s_ya[_ts(t), :], f"dya{t}",
                              waits=[ev_ya_store[-1]] + ([prevd] if prevd else []),
                              ch="dbg")

    es5b = ExitStack()
    e5b = es5b.enter_context
    wdchk = e5b(nc.sbuf_tensor("wdchk", [128, E], F32))
    absb2 = e5b(nc.sbuf_tensor("absb2", [128, E], F32))
    sgnb2 = e5b(nc.sbuf_tensor("sgnb2", [128, E], F32))
    mskb2 = e5b(nc.sbuf_tensor("mskb2", [128, E], F32))
    wqdtm = e5b(nc.sbuf_tensor("wqdtm", [128, FI], BF16))
    wdT = e5b(nc.sbuf_tensor("wdT", [128, NIC * 512], BF16))
    hqf = [e5b(nc.sbuf_tensor(f"hqf{i}", [128, NIC * 128], BF16)) for i in range(2)]
    yb5 = e5b(nc.sbuf_tensor("yb5", [128, 512], F32))
    xb5 = [e5b(nc.sbuf_tensor(f"xb5{i}", [128, 512], F32)) for i in range(2)]
    yab5 = [e5b(nc.sbuf_tensor(f"yab5{i}", [128, 512], F32)) for i in range(2)]
    ob5 = [e5b(nc.sbuf_tensor(f"ob5{i}", [128, 512], F32)) for i in range(2)]
    cdx = cdxp
    ps_y5 = [e5b(nc.psum_tensor(f"ps_y5{i}", [128, 512], F32)) for i in range(2)]
    ps_t5b = [e5b(nc.psum_tensor(f"ps_t5b{i}", [128, 1024], BF16)) for i in range(2)]
    prev_sgn5b = None
    prev_wq5b = None
    hrow_free = None
    prev_zq1 = None
    hq_free = [None, None]
    ob_free = [None, None]
    x5_free = [None, None]
    ya5_free = [None, None]
    y5_free = [None, None]
    yb5_free = None
    ev_hq_store = [None] * TT
    last_out = []
    last_dn_mm = None
    tp5b_free = [None, None]
    ev_wd_done = None
    for foq in range(4):
        for s in range(4):
            r0 = foq * 512 + s * 128
            for cc in range(4):
                c0 = cc * 2048
                c1 = min(FI, c0 + 2048)
                wld = [last_dn_mm] if (s == 0 and cc == 0 and last_dn_mm) else []
                if foq == 0 and s == 0 and cc == 0:
                    wld += p4b_done
                wld += [prev_sgn5b]
                _, ev_w = kb.dma("sync", lambda c0=c0, c1=c1: wdchk[:, 0:c1 - c0],
                                 lambda r0=r0, c0=c0, c1=c1: w_d[r0:r0 + 128, c0:c1],
                                 f"p5bw{foq}_{s}_{cc}", waits=wld, ch="w5b")
                o = kb.op("scalar", lambda eng, c0=c0, c1=c1: eng.activation(
                    absb2[:, 0:c1 - c0], wdchk[:, 0:c1 - c0], AF.Abs,
                    scale=wq_sc["d"]["sw"][:]), f"p5bab{foq}_{s}_{cc}",
                    [ev_w, ev_wsc[("d", "sw")] if foq == 0 else None,
                     prev_wq5b])
                ev_a5 = kb.inc(o, sem_a)
                o = kb.op("scalar", lambda eng, c0=c0, c1=c1: eng.activation(
                    sgnb2[:, 0:c1 - c0], wdchk[:, 0:c1 - c0], AF.Sign),
                    f"p5bsg{foq}_{s}_{cc}")
                ev_s5 = prev_sgn5b = kb.inc(o, sem_a)
                o = kb.op("vector", lambda eng, c0=c0, c1=c1: eng.tensor_scalar(
                    mskb2[:, 0:c1 - c0], absb2[:, 0:c1 - c0], 0.5, None,
                    ALU.is_ge), f"p5bm{foq}_{s}_{cc}", [ev_a5])
                ev = kb.inc(o, sem_v)
                o = kb.op("vector", lambda eng, c0=c0, c1=c1: eng.tensor_tensor(
                    wqdtm[:, c0:c1], mskb2[:, 0:c1 - c0], sgnb2[:, 0:c1 - c0],
                    ALU.mult), f"p5bq{foq}_{s}_{cc}", [ev, ev_s5])
                ev_q5 = prev_wq5b = kb.inc(o, sem_v)
            for ic in range(NIC):
                wic = min(128, FI - ic * 128)
                tpp = ic % 2
                o = kb.op("tensor", lambda eng, ic=ic, wic=wic, tpp=tpp:
                          eng.transpose(ps_t5b[tpp][:wic, 0:128],
                                        wqdtm[:, ic * 128:ic * 128 + wic],
                                        identb[:]),
                          f"p5bt{foq}_{s}_{ic}",
                          [ev_q5, ev_const["identb"], tp5b_free[tpp]])
                ev_tp = kb.inc(o, sem_t)
                o = kb.op("vector", lambda eng, ic=ic, s=s, wic=wic, tpp=tpp:
                          eng.tensor_copy(
                              wdT[:wic, ic * 512 + s * 128:ic * 512 + s * 128 + 128],
                              ps_t5b[tpp][:wic, 0:128]),
                          f"p5bte{foq}_{s}_{ic}", [ev_tp])
                tp5b_free[tpp] = kb.inc(o, sem_v)
                ev_wd_done = tp5b_free[tpp]
        for t in range(TT):
            par = t % 2
            _, ev_hql = kb.dma("sync", lambda par=par: hqf[par][:, 0:NIC * 128],
                               lambda t=t: s_hq[t], f"p5blq{foq}_{t}",
                               waits=[ev_hq_store[t]] +
                               ([hq_free[par]] if hq_free[par] else []),
                               ch="lhq")
            ev_hq_ready = ev_hql
            o = None
            mw5 = [ev_hq_ready, ev_wd_done, y5_free[par]]
            for ic in range(NIC):
                wic = min(128, FI - ic * 128)
                o = kb.op("tensor", lambda eng, ic=ic, wic=wic, par=par:
                          eng.matmul(ps_y5[par][:],
                                     hqf[par][:wic, _ts(ic)],
                                     wdT[:wic, ic * 512:(ic + 1) * 512],
                                     start=(ic == 0), stop=(ic == NIC - 1)),
                          f"p5bdm{foq}_{t}_{ic}", mw5 if ic == 0 else ())
            last_dn_mm = ev_dn = kb.inc(o, sem_t)
            hq_free[par] = ev_dn
            _, ev_x5 = kb.dma("sync", lambda par=par: xb5[par][:],
                              lambda t=t, foq=foq: x_in[_ts(t), _ts(foq, 512)],
                              f"p5blx{foq}_{t}",
                              waits=[x5_free[par]] if x5_free[par] else (),
                              ch="lx5")
            _, ev_ya5 = kb.dma("sync", lambda par=par: yab5[par][:],
                               lambda t=t, foq=foq: s_ya[_ts(t), _ts(foq, 512)],
                               f"p5blya{foq}_{t}",
                               waits=([ev_ya_store[-1]] if (foq == 0 and t < 2) else []) +
                               ([ya5_free[par]] if ya5_free[par] else []),
                               ch="lya")
            o = kb.op("scalar", lambda eng, par=par, t=t: eng.activation(
                yb5[:], ps_y5[par][:], AF.Copy, scale=cdx[:, t:t + 1]),
                f"p5bev{foq}_{t}", [ev_dn, ev_cdx, y5_free[par], yb5_free])
            ev_y = kb.inc(o, sem_a)
            y5_free[par] = ev_y
            o = kb.op("vector", lambda eng, par=par: eng.tensor_tensor(
                ob5[par][:], yb5[:], yab5[par][:], ALU.add), f"p5ba1{foq}_{t}",
                [ev_y, ev_ya5, ob_free[par]])
            ev = yb5_free = kb.inc(o, sem_v)
            ya5_free[par] = ev
            o = kb.op("vector", lambda eng, par=par: eng.tensor_tensor(
                ob5[par][:], ob5[par][:], xb5[par][:], ALU.add), f"p5ba2{foq}_{t}",
                [ev, ev_x5])
            ev_o5 = kb.inc(o, sem_v)
            x5_free[par] = ev_o5
            _, ev_out = kb.dma("gpsimd",
                               lambda t=t, foq=foq: out[_ts(t), _ts(foq, 512)],
                               lambda par=par: ob5[par][:], f"p5bo{foq}_{t}",
                               waits=[ev_o5], ch=f"oo{par}")
            ob_free[par] = ev_out
            last_out.append(ev_out)

    fin_evs = [last_out[-1], last_out[-2], ev_p1_done]
    o = kb.op("gpsimd", lambda eng: eng.engine_nop(), "fin",
              [ev for ev in fin_evs if ev is not None])
    kb.inc(o, sem_g)
    es5b.close()

    kb.emit()
    es.close()
    return nc



_LAST_RESULTS = None


def kernel(L0_attn_x, L0_attn_w_qkv, L0_attn_w_out, L0_attn_ln_scale,
           L0_attn_ln_bias, L0_ffn_w_gate, L0_ffn_w_up, L0_ffn_w_down):
    global _LAST_RESULTS
    debug = tuple(sorted(k for k, v in _DEBUG.items() if v))
    nc = build_nc(debug=debug)
    consts = host_constants()
    x = np.ascontiguousarray(np.asarray(L0_attn_x, dtype=np.float32))
    base = {
        "w_qkv": np.ascontiguousarray(np.asarray(L0_attn_w_qkv, np.float32)),
        "w_out": np.ascontiguousarray(np.asarray(L0_attn_w_out, np.float32)),
        "ln_scale_b": np.ascontiguousarray(
            np.broadcast_to(np.asarray(L0_attn_ln_scale, np.float32)[None, :],
                            (128, 128))),
        "ln_bias_b": np.ascontiguousarray(
            np.broadcast_to(np.asarray(L0_attn_ln_bias, np.float32)[None, :],
                            (128, 128))),
        "w_gate": np.ascontiguousarray(np.asarray(L0_ffn_w_gate, np.float32)),
        "w_up": np.ascontiguousarray(np.asarray(L0_ffn_w_up, np.float32)),
        "w_down": np.ascontiguousarray(np.asarray(L0_ffn_w_down, np.float32)),
        "cos_hi": consts["cos_hi"], "cos_lo": consts["cos_lo"],
        "sin_hi": consts["sin_hi"], "sin_lo": consts["sin_lo"],
        "ident_f32": consts["ident_f32"], "ident_bf16": consts["ident_bf16"],
        "trimask": consts["trimask"],
    }
    in_maps = [dict(base, x=x[b]) for b in range(NC)]
    res = run_bass_kernel_spmd(nc, in_maps, core_ids=list(range(NC)))
    _LAST_RESULTS = res.results
    return np.stack([res.results[b]["out"] for b in range(NC)], axis=0)


# version 11
# speedup vs baseline: 1.0064x; 1.0064x over previous
from contextlib import ExitStack

import numpy as np

import concourse.bass as bass
import concourse.mybir as mybir
from concourse.bass_utils import run_bass_kernel_spmd

F32 = mybir.dt.float32
BF16 = mybir.dt.bfloat16
AF = mybir.ActivationFunctionType
ALU = mybir.AluOpType
AX = mybir.AxisListType

T = 2048
E = 2048
H = 16
D = 128
FQKV = 3 * E
FI = 6464
TT = T // 128
NIC = 51
NC = 8
LN_EPS = 1e-5
QEPS = 1e-5

_DEBUG = {}
_P3_FO_LIMIT = 12
_HQ_NOPUMP = False

ENGS = ("sync", "tensor", "vector", "scalar", "gpsimd")


class Sem:
    __slots__ = ("name", "handle", "count")

    def __init__(self, name):
        self.name = name
        self.handle = None
        self.count = 0


class Op:
    __slots__ = ("fn", "waits", "incs", "seq", "tag")

    def __init__(self, fn, seq, tag):
        self.fn = fn
        self.waits = []
        self.incs = []
        self.seq = seq
        self.tag = tag


class KB:

    def __init__(self, nc):
        self.nc = nc
        self.streams = {e: [] for e in ENGS}
        self.sems = {}
        self.seq = 0

    def sem(self, name):
        if name not in self.sems:
            self.sems[name] = Sem(name)
        return self.sems[name]

    def op(self, eng, fn, tag="", waits=()):
        o = Op(fn, self.seq, tag)
        self.seq += 1
        self.streams[eng].append(o)
        for ev in waits:
            self.wait(o, ev)
        return o

    @staticmethod
    def wait(o, ev):
        if ev is None:
            return o
        sem, val, pseq = ev
        assert pseq < o.seq, f"wait on future event {sem.name}@{val} at {o.tag}"
        for i, (s2, v2, _) in enumerate(o.waits):
            if s2 is sem:
                if val > v2:
                    o.waits[i] = (sem, val, pseq)
                return o
        o.waits.append((sem, val, pseq))
        return o

    @staticmethod
    def inc(o, sem, n=1):
        sem.count += n
        o.incs.append((sem, n, sem.count))
        return (sem, sem.count, o.seq)

    def dma(self, queue_eng, out_ap_fn, in_ap_fn, tag="dma", waits=(), ch=None):
        sem = self.sem("d_" + (ch or queue_eng))
        prev = (sem, sem.count, -1) if sem.count else None

        def fn(eng):
            return eng.dma_start(out=out_ap_fn(), in_=in_ap_fn())

        o = self.op(queue_eng, fn, tag, waits)
        if prev is not None:
            o.waits.append((sem, prev[1], o.seq - 1))
        ev = self.inc(o, sem, 16)
        return o, ev

    def check_inc_order(self):
        seen = {}
        for eng, stream in self.streams.items():
            last_val = {}
            for o in stream:
                for (sem, n, val) in o.incs:
                    prev_eng = seen.setdefault(sem.name, eng)
                    assert prev_eng == eng, \
                        f"sem {sem.name} inc'd from {prev_eng} and {eng}"
                    lv = last_val.get(sem.name)
                    assert lv is None or val > lv, \
                        f"sem {sem.name} inc value {val} at {o.tag} is " \
                        f"stream-later than value {lv} (out-of-order inc)"
                    last_val[sem.name] = val

    def emit(self):
        self.check_inc_order()
        nc = self.nc
        with ExitStack() as es:
            for s in self.sems.values():
                s.handle = es.enter_context(nc.semaphore(s.name))
            block = es.enter_context(nc.Block())

            def render(stream):
                def runner(eng):
                    for o in stream:
                        for (sem, val, _) in o.waits:
                            eng.wait_ge(sem.handle, val)
                        ins = o.fn(eng)
                        for (sem, n, _v) in o.incs:
                            assert ins is not None, f"inc w/o instruction: {o.tag}"
                            ins = ins.then_inc(sem.handle, n)
                return runner

            for ename in ENGS:
                if self.streams[ename]:
                    getattr(block, ename)(render(self.streams[ename]))


def _ts(i, n=128):
    return slice(i * n, (i + 1) * n)


def ceil_div(a, b):
    return -(-a // b)


def _ap3(t2d, off, dim1, dim2):
    a = t2d[:]
    return bass.AP(a.tensor, a.offset + off, [list(a.ap[0]), list(dim1), list(dim2)])


def host_constants():
    import ml_dtypes
    c = {}
    import jax
    with jax.default_device(jax.devices("cpu")[0]):
        import jax.numpy as jnp
        inv = 1.0 / 10000.0 ** (jnp.arange(0, D, 2, dtype=jnp.float32) / D)
        t32 = jnp.arange(T, dtype=jnp.float32)[:, None] * inv[None, :]
        c["cos_hi"] = np.ascontiguousarray(np.asarray(jnp.cos(t32), np.float32))
        c["sin_hi"] = np.ascontiguousarray(np.asarray(jnp.sin(t32), np.float32))
    c["cos_lo"] = np.zeros_like(c["cos_hi"])
    c["sin_lo"] = np.zeros_like(c["sin_hi"])
    c["ident_f32"] = np.eye(128, dtype=np.float32)
    c["ident_bf16"] = np.eye(128, dtype=ml_dtypes.bfloat16)
    c["trimask"] = (np.arange(128)[None, :] >= np.arange(128)[:, None]).astype(np.float32)
    return c



def emit_peel(kb, sems, zb, ubuf, cbuf_, outw, off, width, ev_z, tag,
              extra_waits=(), bias_m128=None):
    sem_v, sem_a, sem_g = sems
    w1 = min(704, max(64, (width // 3) // 64 * 64))
    if width <= 256:
        w1 = width
    rest = width - w1
    wa = (rest // 2) // 64 * 64
    wb = rest - wa
    chains = [("dve", off, w1)]
    if wa:
        chains.append(("agsA", off + w1, wa))
    if wb:
        chains.append(("agsB", off + w1 + wa, wb))
    done = []
    for nm, o0, w in chains:
        if nm != "dve":
            continue
        o = kb.op("vector", lambda eng, o0=o0, w=w: eng.tensor_copy(
            ubuf[:, o0:o0 + w], zb[:, o0:o0 + w]), f"{tag}u",
            [ev_z] + list(extra_waits))
        ev_u = kb.inc(o, sem_v)
        for k in range(7, -1, -1):
            Tk = float(1 << k)
            o = kb.op("vector", lambda eng, Tk=Tk, o0=o0, w=w: eng.tensor_scalar(
                cbuf_[:, o0:o0 + w], ubuf[:, o0:o0 + w], Tk, None, ALU.is_ge),
                f"{tag}c{k}", [ev_u])
            ev_c = kb.inc(o, sem_v)
            o = kb.op("vector", lambda eng, Tk=Tk, o0=o0, w=w:
                      eng.scalar_tensor_tensor(
                          ubuf[:, o0:o0 + w], cbuf_[:, o0:o0 + w], -Tk,
                          ubuf[:, o0:o0 + w], ALU.mult, ALU.add),
                      f"{tag}p{k}", [ev_c])
            ev_u = kb.inc(o, sem_v)
        o = kb.op("vector", lambda eng, o0=o0, w=w: eng.tensor_tensor(
            cbuf_[:, o0:o0 + w], zb[:, o0:o0 + w], ubuf[:, o0:o0 + w],
            ALU.subtract), f"{tag}q1", [ev_u])
        ev = kb.inc(o, sem_v)
        o = kb.op("vector", lambda eng, o0=o0, w=w: eng.tensor_scalar(
            outw(o0, w), cbuf_[:, o0:o0 + w], -128.0, None, ALU.add),
            f"{tag}q2", [ev])
        done.append(kb.inc(o, sem_v))
    ags = [(nm, o0, w) for nm, o0, w in chains if nm != "dve"]
    ev_v = {}
    for nm, o0, w in ags:
        o = kb.op("vector", lambda eng, o0=o0, w=w: eng.tensor_copy(
            ubuf[:, o0:o0 + w], zb[:, o0:o0 + w]), f"{tag}{nm}v",
            [ev_z] + list(extra_waits))
        ev_v[nm] = kb.inc(o, sem_v)
    bias_t, bias_ev = bias_m128
    for k in range(7, -1, -1):
        Tk = float(1 << k)
        ev_s = {}
        for nm, o0, w in ags:
            o = kb.op("scalar", lambda eng, o0=o0, w=w:
                      eng.activation(cbuf_[:, o0:o0 + w], ubuf[:, o0:o0 + w],
                                     AF.Sign, bias=bias_t[:, 0:1]), f"{tag}{nm}s{k}",
                      [ev_v[nm], bias_ev if k == 7 else None])
            ev_s[nm] = kb.inc(o, sem_a)
        for nm, o0, w in ags:
            o = kb.op("vector", lambda eng, o0=o0, w=w, Tk=Tk:
                      eng.scalar_tensor_tensor(
                          ubuf[:, o0:o0 + w], cbuf_[:, o0:o0 + w], -Tk / 2,
                          ubuf[:, o0:o0 + w], ALU.mult, ALU.add),
                      f"{tag}{nm}p{k}", [ev_s[nm]])
            ev_v[nm] = kb.inc(o, sem_v)
    for nm, o0, w in ags:
        o = kb.op("vector", lambda eng, o0=o0, w=w: eng.tensor_tensor(
            cbuf_[:, o0:o0 + w], zb[:, o0:o0 + w], ubuf[:, o0:o0 + w],
            ALU.subtract), f"{tag}{nm}q1", [ev_v[nm]])
        ev = kb.inc(o, sem_v)
        o = kb.op("vector", lambda eng, o0=o0, w=w: eng.tensor_scalar(
            outw(o0, w), cbuf_[:, o0:o0 + w], -0.5, None, ALU.add),
            f"{tag}{nm}q2", [ev])
        done.append(kb.inc(o, sem_v))
    return done



def build_nc(debug=()):
    nc = bass.Bass("TRN2", target_bir_lowering=False, debug=False,
                   detect_race_conditions=False)
    kb = KB(nc)
    dbg = set(debug)

    x_in = nc.dram_tensor("x", [T, E], F32, kind="ExternalInput").ap()
    w_qkv = nc.dram_tensor("w_qkv", [E, FQKV], F32, kind="ExternalInput").ap()
    w_out = nc.dram_tensor("w_out", [E, E], F32, kind="ExternalInput").ap()
    ln_sc = nc.dram_tensor("ln_scale_b", [128, 128], F32, kind="ExternalInput").ap()
    ln_bi = nc.dram_tensor("ln_bias_b", [128, 128], F32, kind="ExternalInput").ap()
    w_g = nc.dram_tensor("w_gate", [FI, E], F32, kind="ExternalInput").ap()
    w_u = nc.dram_tensor("w_up", [FI, E], F32, kind="ExternalInput").ap()
    w_d = nc.dram_tensor("w_down", [E, FI], F32, kind="ExternalInput").ap()
    cos_hi = nc.dram_tensor("cos_hi", [T, 64], F32, kind="ExternalInput").ap()
    cos_lo = nc.dram_tensor("cos_lo", [T, 64], F32, kind="ExternalInput").ap()
    sin_hi = nc.dram_tensor("sin_hi", [T, 64], F32, kind="ExternalInput").ap()
    sin_lo = nc.dram_tensor("sin_lo", [T, 64], F32, kind="ExternalInput").ap()
    ident_f = nc.dram_tensor("ident_f32", [128, 128], F32, kind="ExternalInput").ap()
    ident_b = nc.dram_tensor("ident_bf16", [128, 128], BF16, kind="ExternalInput").ap()
    trimask = nc.dram_tensor("trimask", [128, 128], F32, kind="ExternalInput").ap()

    out = nc.dram_tensor("out", [T, E], F32, kind="ExternalOutput").ap()

    s_xnfm_hi = nc.dram_tensor("s_xnfm_hi", [TT, 128, E], BF16).ap()
    s_xnfm_lo = nc.dram_tensor("s_xnfm_lo", [TT, 128, E], BF16).ap()
    s_xq = nc.dram_tensor("s_xq", [TT, 128, E], BF16).ap()
    s_q_hi = nc.dram_tensor("s_q_hi", [H, 128, T], BF16).ap()
    s_q_lo = nc.dram_tensor("s_q_lo", [H, 128, T], BF16).ap()
    s_k_hi = nc.dram_tensor("s_k_hi", [H, 128, T], BF16).ap()
    s_k_lo = nc.dram_tensor("s_k_lo", [H, 128, T], BF16).ap()
    s_v_hi = nc.dram_tensor("s_v_hi", [H, 128, T], BF16).ap()
    s_v_lo = nc.dram_tensor("s_v_lo", [H, 128, T], BF16).ap()
    s_at_hi = nc.dram_tensor("s_at_hi", [TT, 128, E], BF16).ap()
    s_at_lo = nc.dram_tensor("s_at_lo", [TT, 128, E], BF16).ap()
    s_ya = nc.dram_tensor("s_ya", [T, E], F32).ap()
    s_h = nc.dram_tensor("s_h", [T, FI], F32).ap()
    s_hq = nc.dram_tensor("s_hq", [TT, 128, NIC * 128], BF16).ap()

    dbg_aps = {}

    def dbg_out(name, shape):
        dbg_aps[name] = nc.dram_tensor(name, shape, F32, kind="ExternalOutput").ap()
        return dbg_aps[name]

    es = ExitStack()
    e = es.enter_context

    identf = e(nc.sbuf_tensor("identf", [128, 128], F32))
    identb = e(nc.sbuf_tensor("identb", [128, 128], BF16))
    trim = e(nc.sbuf_tensor("trim", [128, 128], F32))
    lnscb = e(nc.sbuf_tensor("lnscb", [128, 128], F32))
    lnbib = e(nc.sbuf_tensor("lnbib", [128, 128], F32))
    cosb = e(nc.sbuf_tensor("cosb", [128, TT * 64], F32))
    sinb = e(nc.sbuf_tensor("sinb", [128, TT * 64], F32))
    ones_k = e(nc.sbuf_tensor("ones_k", [128, 1], BF16))
    ones_f = e(nc.sbuf_tensor("ones_f", [128, 1], F32))
    ones_row = e(nc.sbuf_tensor("ones_row", [1, 128], F32))
    biasm128 = e(nc.sbuf_tensor("biasm128", [128, 1], F32))
    sxr_all = e(nc.sbuf_tensor("sxr_all", [128, TT], F32))
    csx_all = e(nc.sbuf_tensor("csx_all", [128, TT], F32))
    sxh_all = e(nc.sbuf_tensor("sxh_all", [128, TT], F32))
    csh_all = e(nc.sbuf_tensor("csh_all", [128, TT], F32))
    hmax_all = e(nc.sbuf_tensor("hmax_all", [128, TT], F32))
    cdxp = e(nc.sbuf_tensor("cdxp", [128, TT], F32))
    wq_sc = {nm: {k: e(nc.sbuf_tensor(f"{k}_{nm}", [128, 1], F32))
                  for k in ("cw", "sw")} for nm in ("g", "u", "d")}

    sem_t = kb.sem("tsem")
    sem_v = kb.sem("vsem")
    sem_a = kb.sem("asem")
    sem_g = kb.sem("gsem")

    ev_const = {}
    for nm, dst, src in (("identf", identf, ident_f), ("identb", identb, ident_b),
                         ("trim", trim, trimask), ("lnsc", lnscb, ln_sc),
                         ("lnbi", lnbib, ln_bi)):
        _, ev = kb.dma("sync", lambda d=dst: d[:], lambda s=src: s[:], "c_" + nm, ch="cmisc")
        ev_const[nm] = ev
    ropetmp = e(nc.sbuf_tensor("ropetmp", [128, TT * 64], F32))
    ev_prev_radd = None
    for nm, dstb, srch, srcl in (("cos", cosb, cos_hi, cos_lo),
                                 ("sin", sinb, sin_hi, sin_lo)):
        evs = []
        for ti, t in enumerate(range(TT)):
            _, ev1 = kb.dma("sync", lambda d=dstb, t=t: d[:, t * 64:(t + 1) * 64],
                            lambda s=srch, t=t: s[_ts(t), :], "rh", ch=f"rope{t % 4}")
            _, ev2 = kb.dma("sync", lambda t=t: ropetmp[:, t * 64:(t + 1) * 64],
                            lambda s=srcl, t=t: s[_ts(t), :], "rl",
                            waits=[ev_prev_radd] if (ti == 0 and ev_prev_radd) else (),
                            ch=f"rope{t % 4}")
            evs += [ev1, ev2]
        o = kb.op("vector", lambda eng, d=dstb: eng.tensor_tensor(
            d[:], d[:], ropetmp[:], ALU.add), "radd_" + nm, evs)
        ev_const[nm] = ev_prev_radd = kb.inc(o, sem_v)
    o = kb.op("vector", lambda eng: eng.memset(ones_k[:], 1.0), "ones_k")
    ev_const["ones_k"] = kb.inc(o, sem_v)
    o = kb.op("vector", lambda eng: eng.memset(ones_f[:], 1.0), "ones_f")
    ev_const["ones_f"] = kb.inc(o, sem_v)
    o = kb.op("vector", lambda eng: eng.memset(ones_row[:], 1.0), "ones_row")
    ev_const["ones_row"] = kb.inc(o, sem_v)
    o = kb.op("vector", lambda eng: eng.memset(biasm128[:], -128.0), "bm128")
    ev_const["bm128"] = kb.inc(o, sem_v)

    es12 = ExitStack()
    e12 = es12.enter_context
    wsum = e12(nc.sbuf_tensor("wsum", [128, 3], F32))
    wslab = [e12(nc.sbuf_tensor(f"wslab{i}", [128, 8192], F32)) for i in range(2)]
    wred = e12(nc.sbuf_tensor("wred", [128, 1], F32))
    msc = e12(nc.sbuf_tensor("msc", [1, 8], F32))
    ps_m = e12(nc.psum_tensor("ps_m", [128, 512], F32))

    o = kb.op("vector", lambda eng: eng.memset(wsum[:], 0.0), "p1z")
    ev_p1z = kb.inc(o, sem_v)
    p1_specs = [("g", w_g, FI, E), ("u", w_u, FI, E), ("d", w_d, E, FI)]
    p1_jobs = []
    for wi, (nm, wap, rows, cols) in enumerate(p1_specs):
        for sidx in range(ceil_div(rows, 128)):
            p1_jobs.append((nm, wi, wap, sidx * 128,
                            min(128, rows - sidx * 128), cols))
    p1_state = {"i": 0, "free": [None, None], "mean_evs": {}}

    def p1_pump(n):
        for _ in range(n):
            i = p1_state["i"]
            if i >= len(p1_jobs):
                return
            nm, wi, wap, r0, nr, cols = p1_jobs[i]
            buf = i % 2
            fr = p1_state["free"][buf]
            _, ev_l = kb.dma("sync",
                             lambda b=buf, nr=nr, cols=cols: wslab[b][:nr, :cols],
                             lambda w=wap, r0=r0, nr=nr: w[r0:r0 + nr, :],
                             f"p1l_{nm}{r0}",
                             waits=[fr] if fr else (), ch=f"wsl{buf}")
            o = kb.op("vector", lambda eng, b=buf, nr=nr, cols=cols: eng.reduce_sum(
                wred[:nr, :], wslab[b][:nr, :cols], AX.X,
                apply_absolute_value=True), f"p1r_{nm}{r0}", [ev_l, ev_p1z])
            ev = kb.inc(o, sem_v)
            o = kb.op("vector", lambda eng, nr=nr, wi=wi: eng.tensor_tensor(
                wsum[:nr, wi:wi + 1], wsum[:nr, wi:wi + 1], wred[:nr, :], ALU.add),
                f"p1a_{nm}{r0}", [ev])
            p1_state["free"][buf] = kb.inc(o, sem_v)
            p1_state["mean_evs"][nm] = p1_state["free"][buf]
            p1_state["i"] += 1

    xb = [e12(nc.sbuf_tensor(f"xb{i}", [128, E], F32)) for i in range(2)]
    xnb = e12(nc.sbuf_tensor("xnb", [128, E], F32))
    sqb = e12(nc.sbuf_tensor("sqb", [128, E], F32))
    st1 = e12(nc.sbuf_tensor("st1", [128, 8], F32))
    zb = e12(nc.sbuf_tensor("zb", [128, E], F32))
    ub = e12(nc.sbuf_tensor("ub", [128, E], F32))
    cbuf = e12(nc.sbuf_tensor("cbuf", [128, E], F32))
    xqb = e12(nc.sbuf_tensor("xqb", [128, E], BF16))
    tstg_hi = e12(nc.sbuf_tensor("tstg_hi", [128, E], BF16))
    tstg_lo = e12(nc.sbuf_tensor("tstg_lo", [128, E], BF16))
    tstg_xq = e12(nc.sbuf_tensor("tstg_xq", [128, E], BF16))
    ps_t = [e12(nc.psum_tensor(f"ps_t{i}", [128, 512], F32)) for i in range(2)]
    ps_tb = [e12(nc.psum_tensor(f"ps_tb{i}", [128, 1024], BF16)) for i in range(2)]

    xb_free = [None, None]
    bar = {"v": None, "a": None, "t": None, "st": None}
    ev_p2_store = [None] * TT
    ev_sxr_t = [None] * TT
    for t in range(TT):
        buf = t % 2
        _, ev_x = kb.dma("sync", lambda b=buf: xb[b][:],
                         lambda t=t: x_in[_ts(t), :], f"p2x{t}",
                         waits=[xb_free[buf]] if xb_free[buf] else (),
                         ch=f"x{buf}")
        tile_bar = [bar["v"], bar["a"], bar["t"], bar["st"]]
        o = kb.op("vector", lambda eng, b=buf: eng.reduce_sum(
            st1[:, 0:1], xb[b][:], AX.X), f"p2m{t}", [ev_x] + tile_bar)
        ev = kb.inc(o, sem_v)
        o = kb.op("vector", lambda eng: eng.tensor_scalar(
            st1[:, 1:2], st1[:, 0:1], 1.0 / E, None, ALU.mult), f"p2m2{t}", [ev])
        ev_mean = kb.inc(o, sem_v)
        o = kb.op("scalar", lambda eng, b=buf: eng.activation(
            sqb[:], xb[b][:], AF.Square, accum_out=st1[:, 2:3]), f"p2sq{t}",
            [ev_x] + tile_bar)
        ev_sq = kb.inc(o, sem_a)
        o = kb.op("vector", lambda eng: eng.tensor_tensor(
            st1[:, 3:4], st1[:, 1:2], st1[:, 1:2], ALU.mult), f"p2v1{t}", [ev_mean])
        ev = kb.inc(o, sem_v)
        o = kb.op("vector", lambda eng: eng.scalar_tensor_tensor(
            st1[:, 4:5], st1[:, 2:3], 1.0 / E, st1[:, 3:4], ALU.mult, ALU.subtract),
            f"p2v2{t}", [ev, ev_sq])
        ev = kb.inc(o, sem_v)
        o = kb.op("vector", lambda eng: eng.tensor_scalar(
            st1[:, 5:6], st1[:, 4:5], LN_EPS, None, ALU.add), f"p2v3{t}", [ev])
        ev = kb.inc(o, sem_v)
        o = kb.op("vector", lambda eng: eng.reciprocal(
            st1[:, 6:7], st1[:, 5:6]), f"p2v4{t}", [ev])
        ev = kb.inc(o, sem_v)
        o = kb.op("scalar", lambda eng: eng.activation(
            st1[:, 7:8], st1[:, 6:7], AF.Sqrt), f"p2v5{t}", [ev])
        ev_rstd = kb.inc(o, sem_a)
        o = kb.op("vector", lambda eng, b=buf: eng.tensor_scalar(
            xnb[:], xb[b][:], st1[:, 1:2], st1[:, 7:8], ALU.subtract, ALU.mult),
            f"p2xn{t}", [ev_rstd])
        ev_xn = kb.inc(o, sem_v)
        xb_free[buf] = ev_xn
        o = kb.op("vector", lambda eng: eng.reduce_max(
            st1[:, 0:1], xnb[:], AX.X, apply_absolute_value=True), f"p2am{t}", [ev_xn])
        ev = kb.inc(o, sem_v)
        o = kb.op("vector", lambda eng: eng.tensor_scalar(
            st1[:, 1:2], st1[:, 0:1], QEPS, None, ALU.max), f"p2am2{t}", [ev])
        ev = kb.inc(o, sem_v)
        o = kb.op("vector", lambda eng: eng.reciprocal(st1[:, 2:3], st1[:, 1:2]),
                  f"p2am3{t}", [ev])
        ev = kb.inc(o, sem_v)
        o = kb.op("vector", lambda eng, t=t: eng.tensor_scalar(
            sxr_all[:, t:t + 1], st1[:, 2:3], 127.0, None, ALU.mult), f"p2am4{t}", [ev])
        ev_sxr = kb.inc(o, sem_v)
        ev_sxr_t[t] = ev_sxr
        o = kb.op("vector", lambda eng, t=t: eng.reciprocal(
            csx_all[:, t:t + 1], sxr_all[:, t:t + 1]), f"p2am5{t}", [ev_sxr])
        kb.inc(o, sem_v)
        o = kb.op("scalar", lambda eng, t=t: eng.activation(
            zb[:], xnb[:], AF.Copy, bias=128.5, scale=sxr_all[:, t:t + 1]),
            f"p2z{t}", [ev_sxr])
        ev_z = kb.inc(o, sem_a)
        peel_done = emit_peel(
            kb, (sem_v, sem_a, sem_g), zb, ub, cbuf,
            lambda o0, w: xqb[:, o0:o0 + w], 0, E, ev_z, f"p2pl{t}_",
            extra_waits=tile_bar, bias_m128=(biasm128, ev_const["bm128"]))
        o = kb.op("vector", lambda eng: eng.engine_nop(), f"p2xqn{t}", peel_done)
        ev_xq = kb.inc(o, sem_v)

        pe_ev = {0: None, 1: None}
        pe_evb = {0: None, 1: None}
        last_v = ev_xq
        for j in range(16):
            pb = j % 2
            o = kb.op("tensor", lambda eng, j=j, pb=pb: eng.transpose(
                ps_t[pb][:, 0:128], xnb[:, _ts(j)], identf[:]), f"p2t{t}_{j}",
                [ev_xn, ev_const["identf"], pe_ev[pb]] + (tile_bar if j < 2 else []))
            ev_tp = kb.inc(o, sem_t)
            o = kb.op("vector", lambda eng, j=j, pb=pb: eng.tensor_copy(
                tstg_hi[:, _ts(j)], ps_t[pb][:, 0:128]), f"p2th{t}_{j}", [ev_tp])
            ev = kb.inc(o, sem_v)
            o = kb.op("vector", lambda eng, j=j, pb=pb: eng.tensor_tensor(
                tstg_lo[:, _ts(j)], ps_t[pb][:, 0:128], tstg_hi[:, _ts(j)],
                ALU.subtract), f"p2tl{t}_{j}", [ev])
            pe_ev[pb] = kb.inc(o, sem_v)
            last_v = pe_ev[pb]
        last_xq_tp = None
        for j in range(16):
            pb = j % 2
            o = kb.op("tensor", lambda eng, j=j, pb=pb: eng.transpose(
                ps_tb[pb][:, 0:128], xqb[:, _ts(j)], identb[:]), f"p2tq{t}_{j}",
                [ev_xq, ev_const["identb"], pe_evb[pb]])
            ev_tp = kb.inc(o, sem_t)
            last_xq_tp = ev_tp
            o = kb.op("vector", lambda eng, j=j, pb=pb: eng.tensor_copy(
                tstg_xq[:, _ts(j)], ps_tb[pb][:, 0:128]), f"p2tqe{t}_{j}", [ev_tp])
            pe_evb[pb] = kb.inc(o, sem_v)
            last_v = pe_evb[pb]

        evs_st = []
        for ki, (stg, tgt) in enumerate(((tstg_hi, s_xnfm_hi), (tstg_lo, s_xnfm_lo),
                                         (tstg_xq, s_xq))):
            _, ev_s = kb.dma("gpsimd", lambda tg=tgt, t=t: tg[t],
                             lambda s=stg: s[:], f"p2st{t}",
                             waits=[last_v], ch=f"st2{ki}")
            evs_st.append(ev_s)
        ev_p2_store[t] = evs_st
        bar = {"v": last_v, "a": ev_z, "t": last_xq_tp, "st": evs_st[-1]}

        p1_pump(8)
        if "xn" in dbg:
            ap = dbg_aps.get("dbg_xn") or dbg_out("dbg_xn", [T, E])
            kb.dma("gpsimd", lambda t=t, a=ap: a[_ts(t), :], lambda: xnb[:],
                   f"dxn{t}", waits=[last_v], ch="dbg")
        if "xq" in dbg:
            ap = dbg_aps.get("dbg_xq") or dbg_out("dbg_xq", [T, E])
            o, _ = kb.dma("gpsimd", lambda t=t, a=ap: a[_ts(t), :], lambda: cbuf[:],
                          f"dxq{t}", waits=[ev_xq], ch="dbg")

    p1_pump(len(p1_jobs))
    counts = {"g": FI * E, "u": FI * E, "d": E * FI}
    for wi, nm in enumerate(("g", "u", "d")):
        o = kb.op("tensor", lambda eng, wi=wi: eng.matmul(
            ps_m[:1, 0:1], wsum[:, wi:wi + 1], ones_f[:], start=True, stop=True),
            f"p1mm_{nm}", [p1_state["mean_evs"][nm], ev_const["ones_f"]])
        if wi > 0:
            kb.wait(o, ev_cp)
        ev_mm = kb.inc(o, sem_t)
        o = kb.op("vector", lambda eng, wi=wi: eng.tensor_copy(
            msc[:1, wi:wi + 1], ps_m[:1, 0:1]), f"p1c_{nm}", [ev_mm])
        ev_cp = kb.inc(o, sem_v)
    o = kb.op("vector", lambda eng: eng.tensor_scalar(
        msc[:1, 3:6], msc[:1, 0:3], 1.0 / counts["g"], QEPS, ALU.mult, ALU.max),
        "p1s", [ev_cp])
    ev = kb.inc(o, sem_v)
    o = kb.op("vector", lambda eng: eng.reciprocal(msc[:1, 6:7], msc[:1, 3:4]), "p1r0", [ev])
    ev = kb.inc(o, sem_v)
    ev_wsc = {}
    ev_prev_cp = ev
    for wi, nm in enumerate(("g", "u", "d")):
        o = kb.op("vector", lambda eng, wi=wi: eng.reciprocal(
            msc[:1, 6:7], msc[:1, 3 + wi:4 + wi]), f"p1rc_{nm}", [ev_prev_cp])
        ev_r = kb.inc(o, sem_v)
        for key, col in (("cw", 3 + wi), ("sw", 6)):
            o = kb.op("tensor", lambda eng, col=col: eng.matmul(
                ps_m[:, 0:1], ones_row[:1, :], msc[:1, col:col + 1],
                start=True, stop=True), f"p1b_{nm}{key}",
                [ev_r, ev_const["ones_row"]])
            if ev_wsc:
                kb.wait(o, last_bc_cp)
            ev_b = kb.inc(o, sem_t)
            o = kb.op("vector", lambda eng, nm=nm, key=key: eng.tensor_copy(
                wq_sc[nm][key][:], ps_m[:, 0:1]), f"p1bc_{nm}{key}", [ev_b])
            last_bc_cp = kb.inc(o, sem_v)
            ev_wsc[(nm, key)] = last_bc_cp
        ev_prev_cp = last_bc_cp
    ev_p1_done = last_bc_cp

    if "means" in dbg:
        ap = dbg_out("dbg_means", [1, 8])
        kb.dma("gpsimd", lambda a=ap: a[:, :], lambda: msc[:, :], "dmeans",
               waits=[ev_p1_done], ch="dbg")

    es12.close()
    p2bar = [bar["v"], bar["a"], bar["t"], bar["st"], ev_p1_done]

    es5a = ExitStack()
    e5a = es5a.enter_context
    xqr = [e5a(nc.sbuf_tensor(f"xqr{i}", [128, E], BF16)) for i in range(TT)]
    wslab5 = e5a(nc.sbuf_tensor("wslab5", [128, E], F32))
    absb = e5a(nc.sbuf_tensor("absb", [128, E], F32))
    sgnb = e5a(nc.sbuf_tensor("sgnb", [128, E], F32))
    mskb = e5a(nc.sbuf_tensor("mskb", [128, E], F32))
    wqtm = e5a(nc.sbuf_tensor("wqtm", [128, E], BF16))
    wgT = e5a(nc.sbuf_tensor("wgT", [128, 16 * 512], BF16))
    wuT = e5a(nc.sbuf_tensor("wuT", [128, 16 * 512], BF16))
    gb5 = e5a(nc.sbuf_tensor("gb5", [128, 512], F32))
    ub5a = e5a(nc.sbuf_tensor("ub5a", [128, 512], F32))
    hb5 = [e5a(nc.sbuf_tensor(f"hb5{i}", [128, 512], F32)) for i in range(2)]
    hm5 = e5a(nc.sbuf_tensor("hm5", [128, 1], F32))
    cgx = e5a(nc.sbuf_tensor("cgx", [128, TT], F32))
    cux = e5a(nc.sbuf_tensor("cux", [128, TT], F32))
    ps_g = [e5a(nc.psum_tensor(f"ps_g{i}", [128, 512], F32)) for i in range(2)]
    ps_u = [e5a(nc.psum_tensor(f"ps_u{i}", [128, 512], F32)) for i in range(2)]
    ps_t5 = [e5a(nc.psum_tensor(f"ps_t5{i}", [128, 1024], BF16)) for i in range(2)]

    o = kb.op("vector", lambda eng: eng.memset(hmax_all[:], 0.0), "p5hz",
              p2bar)
    ev_hz = kb.inc(o, sem_v)
    o = kb.op("vector", lambda eng: eng.tensor_scalar(
        cgx[:], csx_all[:], wq_sc["g"]["cw"][:], None, ALU.mult), "p5cg",
        [ev_p1_done, ev_sxr_t[-1]])
    ev_cg = kb.inc(o, sem_v)
    o = kb.op("vector", lambda eng: eng.tensor_scalar(
        cux[:], csx_all[:], wq_sc["u"]["cw"][:], None, ALU.mult), "p5cu", [ev_cg])
    ev_cu = kb.inc(o, sem_v)

    ev_xq_t = []
    for t in range(TT):
        _, ev_xq5 = kb.dma("sync", lambda t=t: xqr[t][:], lambda t=t: s_xq[t],
                           f"p5xq{t}",
                           waits=[ev_p2_store[t][2]] + (p2bar if t == 0 else []),
                           ch=f"xq5{t % 2}")
        ev_xq_t.append(ev_xq5)
    ev_xq_all = ev_xq5

    O_CH = [(oc * 512, min(FI, (oc + 1) * 512)) for oc in range(13)]
    prev_sgn5 = None
    prev_wq5 = None
    hst_free = [None, None]
    tp5_free = [None, None]
    gq_free = None
    last_g_mm = None
    ev_h_store = []
    for oc, (o0, o1) in enumerate(O_CH):
        woc = o1 - o0
        nslab = ceil_div(woc, 128)
        ev_wq_done = None
        for wnm, wap, wT, swt in (("g", w_g, wgT, "g"), ("u", w_u, wuT, "u")):
            for s in range(nslab):
                r0 = o0 + s * 128
                nr = min(128, o1 - r0)
                wld = [last_g_mm] if (s == 0 and wnm == "g" and last_g_mm) else []
                if oc == 0 and s == 0 and wnm == "g":
                    wld += [ev_xq_all]
                wld += [prev_sgn5]
                _, ev_w = kb.dma("sync", lambda nr=nr: wslab5[:nr, :],
                                 lambda wap=wap, r0=r0, nr=nr: wap[r0:r0 + nr, :],
                                 f"p5w{wnm}{oc}_{s}", waits=wld, ch=f"w5{wnm}")
                o = kb.op("scalar", lambda eng, nr=nr, swt=swt: eng.activation(
                    absb[:nr, :], wslab5[:nr, :], AF.Abs,
                    scale=wq_sc[swt]["sw"][:nr, :]), f"p5ab{wnm}{oc}_{s}",
                    [ev_w, ev_wsc[(swt, "sw")] if oc == 0 else None,
                     prev_wq5])
                ev_a5 = kb.inc(o, sem_a)
                o = kb.op("scalar", lambda eng, nr=nr: eng.activation(
                    sgnb[:nr, :], wslab5[:nr, :], AF.Sign), f"p5sg{wnm}{oc}_{s}")
                ev_s5 = prev_sgn5 = kb.inc(o, sem_a)
                o = kb.op("vector", lambda eng, nr=nr: eng.tensor_scalar(
                    mskb[:nr, :], absb[:nr, :], 0.5, None, ALU.is_ge),
                    f"p5m{wnm}{oc}_{s}", [ev_a5])
                ev = kb.inc(o, sem_v)
                o = kb.op("vector", lambda eng, nr=nr: eng.tensor_tensor(
                    wqtm[:nr, :], mskb[:nr, :], sgnb[:nr, :], ALU.mult),
                    f"p5q{wnm}{oc}_{s}", [ev, ev_s5])
                ev_q5 = prev_wq5 = kb.inc(o, sem_v)
                for fi in range(16):
                    tpp = fi % 2
                    o = kb.op("tensor", lambda eng, fi=fi, nr=nr, tpp=tpp:
                              eng.transpose(ps_t5[tpp][:, 0:nr],
                                            wqtm[:nr, _ts(fi)], identb[:nr, :nr]),
                              f"p5t{wnm}{oc}_{s}_{fi}",
                              [ev_q5, ev_const["identb"], tp5_free[tpp]])
                    ev_tp = kb.inc(o, sem_t)
                    o = kb.op("vector", lambda eng, fi=fi, s=s, nr=nr, tpp=tpp,
                              wT=wT: eng.tensor_copy(
                                  wT[:, fi * 512 + s * 128:fi * 512 + s * 128 + nr],
                                  ps_t5[tpp][:, 0:nr]),
                              f"p5te{wnm}{oc}_{s}_{fi}", [ev_tp])
                    tp5_free[tpp] = kb.inc(o, sem_v)
                    ev_wq_done = tp5_free[tpp]
        for t in range(TT):
            par = t % 2
            mw = [ev_wq_done, hst_free[par], ev_xq_t[t]]
            o = None
            for fi in range(16):
                o = kb.op("tensor", lambda eng, t=t, fi=fi, par=par, woc=woc:
                          eng.matmul(ps_g[par][:, 0:woc], xqr[t][:, _ts(fi)],
                                     wgT[:, fi * 512:fi * 512 + woc],
                                     start=(fi == 0), stop=(fi == 15)),
                          f"p5mg{oc}_{t}_{fi}", mw if fi == 0 else ())
                o = kb.op("tensor", lambda eng, t=t, fi=fi, par=par, woc=woc:
                          eng.matmul(ps_u[par][:, 0:woc], xqr[t][:, _ts(fi)],
                                     wuT[:, fi * 512:fi * 512 + woc],
                                     start=(fi == 0), stop=(fi == 15)),
                          f"p5mu{oc}_{t}_{fi}", ())
            last_g_mm = ev_mm5 = kb.inc(o, sem_t)
            o = kb.op("scalar", lambda eng, par=par, t=t, woc=woc: eng.activation(
                gb5[:, 0:woc], ps_g[par][:, 0:woc], AF.Silu,
                scale=cgx[:, t:t + 1]), f"p5eg{oc}_{t}",
                [ev_mm5, ev_cg, gq_free])
            ev_g = kb.inc(o, sem_a)
            o = kb.op("scalar", lambda eng, par=par, t=t, woc=woc: eng.activation(
                ub5a[:, 0:woc], ps_u[par][:, 0:woc], AF.Copy,
                scale=cux[:, t:t + 1]), f"p5eu{oc}_{t}", [ev_cu])
            ev_u = kb.inc(o, sem_a)
            o = kb.op("vector", lambda eng, par=par, woc=woc: eng.tensor_tensor(
                hb5[par][:, 0:woc], gb5[:, 0:woc], ub5a[:, 0:woc], ALU.mult),
                f"p5h{oc}_{t}", [ev_g, ev_u,
                                 ev_h_store[-2][1] if len(ev_h_store) >= 2 else None])
            hst_free[par] = ev_h5 = kb.inc(o, sem_v)
            gq_free = ev_h5
            o = kb.op("vector", lambda eng, par=par, woc=woc: eng.reduce_max(
                hm5[:], hb5[par][:, 0:woc], AX.X, apply_absolute_value=True),
                f"p5hm{oc}_{t}", [ev_h5, ev_hz])
            ev = kb.inc(o, sem_v)
            o = kb.op("vector", lambda eng, t=t: eng.tensor_tensor(
                hmax_all[:, t:t + 1], hmax_all[:, t:t + 1], hm5[:], ALU.max),
                f"p5hx{oc}_{t}", [ev])
            kb.inc(o, sem_v)
            _, ev_sh = kb.dma("gpsimd",
                              lambda t=t, o0=o0, o1=o1: s_h[_ts(t), o0:o1],
                              lambda par=par, woc=woc: hb5[par][:, 0:woc],
                              f"p5sh{oc}_{t}", waits=[ev_h5], ch=f"sh{par}")
            ev_h_store.append((f"{oc}_{t}", ev_sh))
    p5a_done = [ev_h_store[-1][1], ev_h_store[-2][1], last_g_mm, ev_wq_done,
                tp5_free[0], tp5_free[1]]
    o = kb.op("vector", lambda eng: eng.tensor_scalar(
        sxh_all[:], hmax_all[:], QEPS, None, ALU.max), "p5bsx",
        p5a_done)
    ev = kb.inc(o, sem_v)
    o = kb.op("vector", lambda eng: eng.reciprocal(csh_all[:], sxh_all[:]),
              "p5brc", [ev])
    ev = kb.inc(o, sem_v)
    o = kb.op("vector", lambda eng: eng.tensor_scalar(
        sxh_all[:], csh_all[:], 127.0, None, ALU.mult), "p5bs2", [ev])
    ev_sxh = kb.inc(o, sem_v)
    o = kb.op("vector", lambda eng: eng.reciprocal(csh_all[:], sxh_all[:]),
              "p5brc2", [ev_sxh])
    ev_csh = kb.inc(o, sem_v)
    o = kb.op("vector", lambda eng: eng.tensor_scalar(
        cdxp[:], csh_all[:], wq_sc["d"]["cw"][:], None, ALU.mult), "p5bcd",
        [ev_csh, ev_wsc[("d", "cw")]])
    ev_cdx = kb.inc(o, sem_v)
    es5a.close()

    if "h" in dbg:
        ap = dbg_out("dbg_h", [T, FI])
        prevd = None
        for t in range(TT):
            _, prevd = kb.dma("gpsimd", lambda a=ap, t=t: a[_ts(t), :],
                              lambda t=t: s_h[_ts(t), :], f"dh{t}",
                              waits=[ev_h_store[-1][1]] + ([prevd] if prevd else []),
                              ch="dbg")


    es3 = ExitStack()
    e3 = es3.enter_context
    xlh = [e3(nc.sbuf_tensor(f"xlh{i}", [128, E], BF16)) for i in range(8)]
    xll = [e3(nc.sbuf_tensor(f"xll{i}", [128, E], BF16)) for i in range(8)]
    wstg = e3(nc.sbuf_tensor("wstg", [128, 8 * 512], F32))
    whib = [e3(nc.sbuf_tensor(f"whib{i}", [128, 16 * 512], BF16)) for i in range(2)]
    wlob = [e3(nc.sbuf_tensor(f"wlob{i}", [128, 16 * 512], BF16)) for i in range(2)]
    qevb = [e3(nc.sbuf_tensor(f"qevb{i}", [128, 512], F32)) for i in range(3)]
    rbuf = e3(nc.sbuf_tensor("rbuf", [128, 512], F32))
    nbuf = e3(nc.sbuf_tensor("nbuf", [128, 512], F32))
    tmpb = e3(nc.sbuf_tensor("tmpb", [128, 256], F32))
    sqw = e3(nc.sbuf_tensor("sqw", [128, 512], F32))
    hst = e3(nc.sbuf_tensor("hst", [128, 32], F32))
    qhib = [e3(nc.sbuf_tensor(f"qhib{i}", [128, 512], BF16)) for i in range(2)]
    qlob = [e3(nc.sbuf_tensor(f"qlob{i}", [128, 512], BF16)) for i in range(2)]
    stg3_hi = e3(nc.sbuf_tensor("stg3_hi", [128, 4 * 1024], BF16))
    stg3_lo = e3(nc.sbuf_tensor("stg3_lo", [128, 4 * 1024], BF16))
    ps_qkv = [e3(nc.psum_tensor(f"ps_qkv{i}", [128, 512], F32)) for i in range(3)]
    ps_tp3 = [e3(nc.psum_tensor(f"ps_tp3{i}", [128, 1024], BF16)) for i in range(2)]

    qk_targets = {0: (s_q_hi, s_q_lo), 1: (s_k_hi, s_k_lo)}
    ev_qkv_store = {"q": {}, "k": {}, "v": {}}

    ci = 0
    ps_free = [None, None, None]
    qb_free = [None, None, None]
    qsplit_free = [None, None]
    pending_tp = []
    chunk_last_ev = None
    chunk_last_ev2 = None

    def flush_tp():
        nonlocal stg_free3, tp_free, qsplit_free, pending_tp
        nonlocal chunk_last_ev, chunk_last_ev2
        for (cj, qp, t8j, foj, ev_hi, ev_split) in pending_tp:
            stg_hi_t, stg_lo_t = stg3_hi, stg3_lo
            last_tp = None
            for src_, stg, pfx, evr in ((qhib[qp], stg_hi_t, "h", ev_hi),
                                        (qlob[qp], stg_lo_t, "l", ev_split)):
                for hh in range(4):
                    tpar = (cj * 8 + (0 if pfx == "h" else 4) + hh) % 2
                    o = kb.op("tensor", lambda eng, src_=src_, hh=hh, tpar=tpar:
                              eng.transpose(ps_tp3[tpar][:, 0:128],
                                            src_[:, hh * 128:(hh + 1) * 128],
                                            identb[:]),
                              f"p3tp{cj}{pfx}{hh}",
                              [evr, ev_const["identb"], tp_free[tpar]])
                    ev_tp = kb.inc(o, sem_t)
                    last_tp = ev_tp
                    dst = lambda stg=stg, hh=hh, t8j=t8j: stg[
                        :, hh * 1024 + t8j * 128:hh * 1024 + t8j * 128 + 128]
                    if hh % 2 == 0:
                        o = kb.op("vector", lambda eng, dst=dst, tpar=tpar:
                                  eng.tensor_copy(dst(), ps_tp3[tpar][:, 0:128]),
                                  f"p3te{cj}{pfx}{hh}",
                                  [ev_tp] + (stg_free3 if (t8j == 0 and stg_free3) else []))
                        tp_free[tpar] = kb.inc(o, sem_v)
                        chunk_last_ev = tp_free[tpar]
                    else:
                        o = kb.op("scalar", lambda eng, dst=dst, tpar=tpar:
                                  eng.activation(dst(), ps_tp3[tpar][:, 0:128],
                                                 AF.Copy),
                                  f"p3te{cj}{pfx}{hh}",
                                  [ev_tp] + (stg_free3 if (t8j == 0 and stg_free3) else []))
                        tp_free[tpar] = kb.inc(o, sem_a)
                        chunk_last_ev2 = tp_free[tpar]
            qsplit_free[qp] = last_tp
        pending_tp = []
    tp_free = [None, None]
    stg_free3 = []
    nbuf_free = None
    sqw_free = None
    lhsT_free = None
    for half in (0, 1):
        ev_lhsT = []
        for t8 in range(8):
            t = half * 8 + t8
            w0 = [ev_p2_store[t][0]] \
                + ((p2bar + p5a_done) if (half == 0 and t8 == 0) else []) \
                + ([lhsT_free] if (half == 1 and t8 == 0) else [])
            _, evh = kb.dma("sync", lambda i=t8: xlh[i][:],
                            lambda t=t: s_xnfm_hi[t], f"p3lh{t}", waits=w0,
                            ch=f"lh{t8 % 2}")
            _, evl = kb.dma("sync", lambda i=t8: xll[i][:],
                            lambda t=t: s_xnfm_lo[t],
                            f"p3ll{t}", waits=[ev_p2_store[t][1]],
                            ch=f"ll{t8 % 2}")
            ev_lhsT += [evh, evl]
        last_mm = None
        last_mm_wpar = [None, None]
        wsplit_ev = [None, None]
        wstg_free = None

        def emit_wload(fo, extra=()):
            nonlocal wsplit_ev, wstg_free
            wpar = fo % 2
            for sub in range(2):
                ev_wl = []
                for g in range(2):
                    gg = sub * 2 + g
                    wget = list(extra) if (sub == 0 and g == 0) else []
                    if wstg_free is not None and g == 0:
                        wget.append(wstg_free)
                    _, evw = kb.dma(
                        "sync",
                        lambda g=g: wstg[:, g * 4 * 512:(g + 1) * 4 * 512],
                        lambda gg=gg, fo=fo: w_qkv[:, fo * 512:(fo + 1) * 512]
                        .rearrange("(fi p) n -> p fi n", p=128)[:, gg * 4:(gg + 1) * 4, :],
                        f"p3w{half}_{fo}_{gg}", waits=wget, ch=f"w3{gg}")
                    ev_wl.append(evw)
                wl = [last_mm_wpar[wpar]] if (sub == 0 and last_mm_wpar[wpar]) else []
                o = kb.op("vector", lambda eng, sub=sub, wpar=wpar: eng.tensor_copy(
                    whib[wpar][:, sub * 4096:(sub + 1) * 4096], wstg[:]),
                    f"p3wh{half}_{fo}_{sub}", ev_wl + wl)
                ev = kb.inc(o, sem_v)
                o = kb.op("vector", lambda eng, sub=sub, wpar=wpar: eng.tensor_tensor(
                    wlob[wpar][:, sub * 4096:(sub + 1) * 4096], wstg[:],
                    whib[wpar][:, sub * 4096:(sub + 1) * 4096], ALU.subtract),
                    f"p3wl{half}_{fo}_{sub}", [ev])
                wstg_free = kb.inc(o, sem_v)
            wsplit_ev[wpar] = wstg_free

        emit_wload(0, extra=((p2bar + p5a_done) if half == 0 else [lhsT_free]))
        for fo in range(_P3_FO_LIMIT):
            wpar = fo % 2
            if fo + 1 < _P3_FO_LIMIT:
                emit_wload(fo + 1)
            wsplit_done = wsplit_ev[wpar]
            for t8 in range(8):
                t = half * 8 + t8
                par = ci % 3
                mm_waits = [wsplit_done, ps_free[par]] + ev_lhsT
                o = None
                for fi in range(16):
                    for (lt, rt) in ((xlh[t8], whib[wpar]), (xlh[t8], wlob[wpar]),
                                     (xll[t8], whib[wpar])):
                        first = (fi == 0 and lt is xlh[t8] and rt is whib[wpar])
                        last = (fi == 15 and lt is xll[t8])
                        o = kb.op("tensor", lambda eng, lt=lt, rt=rt, fi=fi, par=par,
                                  first=first, last=last: eng.matmul(
                                      ps_qkv[par][:], lt[:, _ts(fi)],
                                      rt[:, fi * 512:(fi + 1) * 512],
                                      start=first, stop=last),
                                  f"p3mm{half}_{fo}_{t8}_{fi}",
                                  mm_waits if first else ())
                ev_mm = kb.inc(o, sem_t)
                last_mm = ev_mm
                last_mm_wpar[wpar] = ev_mm
                flush_tp()
                if fo < 8:
                    kind = 0 if fo < 4 else 1
                    h0 = (fo % 4) * 4
                    o = kb.op("scalar", lambda eng, par=par: eng.activation(
                        qevb[par][:], ps_qkv[par][:], AF.Copy),
                        f"p3ev{half}_{fo}_{t8}", [ev_mm, qb_free[par]])
                    ev_qb = kb.inc(o, sem_a)
                    ps_free[par] = ev_qb
                    H4 = [128, 4]
                    J64 = [1, 64]
                    x1 = _ap3(qevb[par], 0, H4, J64)
                    x2 = _ap3(qevb[par], 64, H4, J64)
                    r1 = _ap3(rbuf, 0, H4, J64)
                    r2 = _ap3(rbuf, 64, H4, J64)
                    t4 = _ap3(tmpb, 0, [64, 4], J64)
                    cosT = _ap3(cosb, t * 64, [0, 4], J64)
                    sinT = _ap3(sinb, t * 64, [0, 4], J64)
                    o = kb.op("vector", lambda eng, r1=r1, x1=x1, cosT=cosT:
                              eng.tensor_tensor(r1, x1, cosT, ALU.mult),
                              f"p3r1{ci}", [ev_qb, ev_const["cos"],
                                            ev_const["sin"], nbuf_free])
                    ev = kb.inc(o, sem_v)
                    o = kb.op("vector", lambda eng, t4=t4, x2=x2, sinT=sinT:
                              eng.tensor_tensor(t4, x2, sinT, ALU.mult),
                              f"p3r2{ci}", [ev])
                    ev = kb.inc(o, sem_v)
                    o = kb.op("vector", lambda eng, r1=r1, t4=t4:
                              eng.tensor_tensor(r1, r1, t4, ALU.subtract),
                              f"p3r3{ci}", [ev])
                    ev = kb.inc(o, sem_v)
                    o = kb.op("vector", lambda eng, r2=r2, x1=x1, sinT=sinT:
                              eng.tensor_tensor(r2, x1, sinT, ALU.mult),
                              f"p3r4{ci}", [ev])
                    ev = kb.inc(o, sem_v)
                    o = kb.op("vector", lambda eng, t4=t4, x2=x2, cosT=cosT:
                              eng.tensor_tensor(t4, x2, cosT, ALU.mult),
                              f"p3r5{ci}", [ev])
                    ev = kb.inc(o, sem_v)
                    o = kb.op("vector", lambda eng, r2=r2, t4=t4:
                              eng.tensor_tensor(r2, r2, t4, ALU.add),
                              f"p3r6{ci}", [ev])
                    last_rope = kb.inc(o, sem_v)
                    qb_free[par] = last_rope
                    r3 = _ap3(rbuf, 0, H4, [1, 128])
                    n3 = _ap3(nbuf, 0, H4, [1, 128])
                    o = kb.op("vector", lambda eng, r3=r3: eng.reduce_sum(
                        hst[:, 0:4], r3, AX.X), f"p3l1{ci}", [last_rope])
                    ev = kb.inc(o, sem_v)
                    o = kb.op("vector", lambda eng: eng.tensor_scalar(
                        hst[:, 4:8], hst[:, 0:4], 1.0 / 128, None, ALU.mult),
                        f"p3l2{ci}", [ev])
                    ev_mean = kb.inc(o, sem_v)
                    o = kb.op("vector", lambda eng: eng.tensor_tensor(
                        sqw[:], rbuf[:], rbuf[:], ALU.mult), f"p3l3{ci}",
                        [last_rope])
                    ev = kb.inc(o, sem_v)
                    o = kb.op("vector", lambda eng: eng.reduce_sum(
                        hst[:, 8:12], _ap3(sqw, 0, H4, [1, 128]), AX.X),
                        f"p3l4{ci}", [ev])
                    ev = kb.inc(o, sem_v)
                    o = kb.op("vector", lambda eng: eng.tensor_tensor(
                        hst[:, 12:16], hst[:, 4:8], hst[:, 4:8], ALU.mult),
                        f"p3l5{ci}", [ev_mean])
                    ev = kb.inc(o, sem_v)
                    o = kb.op("vector", lambda eng: eng.scalar_tensor_tensor(
                        hst[:, 16:20], hst[:, 8:12], 1.0 / 128, hst[:, 12:16],
                        ALU.mult, ALU.subtract), f"p3l6{ci}", [ev])
                    ev = kb.inc(o, sem_v)
                    o = kb.op("vector", lambda eng: eng.tensor_scalar(
                        hst[:, 20:24], hst[:, 16:20], LN_EPS, None, ALU.add),
                        f"p3l7{ci}", [ev])
                    ev = kb.inc(o, sem_v)
                    o = kb.op("vector", lambda eng: eng.reciprocal(
                        hst[:, 24:28], hst[:, 20:24]), f"p3l8{ci}", [ev])
                    ev = kb.inc(o, sem_v)
                    o = kb.op("scalar", lambda eng: eng.activation(
                        hst[:, 28:32], hst[:, 24:28], AF.Sqrt), f"p3l9{ci}", [ev])
                    ev_rs = kb.inc(o, sem_a)
                    mean_b = _ap3(hst, 4, [1, 4], [0, 128])
                    rstd_b = _ap3(hst, 28, [1, 4], [0, 128])
                    lnsc_b = _ap3(lnscb, 0, [0, 4], [1, 128])
                    lnbi_b = _ap3(lnbib, 0, [0, 4], [1, 128])
                    o = kb.op("vector", lambda eng, n3=n3, r3=r3, mean_b=mean_b:
                              eng.tensor_tensor(n3, r3, mean_b, ALU.subtract),
                              f"p3lA{ci}", [ev_rs])
                    ev = kb.inc(o, sem_v)
                    o = kb.op("vector", lambda eng, n3=n3, rstd_b=rstd_b:
                              eng.tensor_tensor(n3, n3, rstd_b, ALU.mult),
                              f"p3lB{ci}", [ev])
                    ev = kb.inc(o, sem_v)
                    o = kb.op("vector", lambda eng, n3=n3, lnsc_b=lnsc_b:
                              eng.tensor_tensor(n3, n3, lnsc_b, ALU.mult),
                              f"p3lC{ci}", [ev, ev_const["lnsc"], ev_const["lnbi"]])
                    ev = kb.inc(o, sem_v)
                    o = kb.op("vector", lambda eng, n3=n3, lnbi_b=lnbi_b:
                              eng.tensor_tensor(n3, n3, lnbi_b, ALU.add),
                              f"p3lD{ci}", [ev])
                    ev_lnd = kb.inc(o, sem_v)
                    qp = ci % 2
                    o = kb.op("scalar", lambda eng, qp=qp: eng.activation(
                        qhib[qp][:], nbuf[:], AF.Copy), f"p3hi{ci}",
                        [ev_lnd, qsplit_free[qp]])
                    ev_hi = kb.inc(o, sem_a)
                    o = kb.op("vector", lambda eng, qp=qp: eng.tensor_tensor(
                        qlob[qp][:], nbuf[:], qhib[qp][:], ALU.subtract),
                        f"p3lo{ci}", [ev_hi])
                    ev_split = kb.inc(o, sem_v)
                    nbuf_free = ev_split
                    pending_tp.append((ci, qp, t8, fo, ev_hi, ev_split))
                    if "qkv" in dbg or "qk" in dbg:
                        nmq = "dbg_q" if kind == 0 else "dbg_k"
                        ap = dbg_aps.get(nmq) or dbg_out(nmq, [T, E])
                        o, evd = kb.dma("gpsimd",
                                        lambda a=ap, t=t, fo=fo: a[_ts(t), (fo % 4) * 512:(fo % 4) * 512 + 512],
                                        lambda: nbuf[:], f"d3{ci}", waits=[ev_split],
                                        ch="dbg")
                        nbuf_free = evd
                else:
                    h0 = (fo - 8) * 4
                    evs_c = []
                    for hh in range(4):
                        dst = lambda stg, hh=hh, t8=t8: stg[:, hh * 1024 + t8 * 128:
                                                           hh * 1024 + t8 * 128 + 128]
                        o = kb.op("vector", lambda eng, par=par, hh=hh, t8=t8:
                                  eng.tensor_copy(
                                      stg3_hi[:, hh * 1024 + t8 * 128:hh * 1024 + t8 * 128 + 128],
                                      ps_qkv[par][:, hh * 128:(hh + 1) * 128]),
                                  f"p3vh{ci}_{hh}",
                                  [ev_mm] + (stg_free3 if (t8 == 0 and hh == 0 and stg_free3) else []))
                        ev = kb.inc(o, sem_v)
                        o = kb.op("vector", lambda eng, par=par, hh=hh, t8=t8:
                                  eng.tensor_tensor(
                                      stg3_lo[:, hh * 1024 + t8 * 128:hh * 1024 + t8 * 128 + 128],
                                      ps_qkv[par][:, hh * 128:(hh + 1) * 128],
                                      stg3_hi[:, hh * 1024 + t8 * 128:hh * 1024 + t8 * 128 + 128],
                                      ALU.subtract), f"p3vl{ci}_{hh}", [ev])
                        evs_c.append(kb.inc(o, sem_v))
                    ps_free[par] = evs_c[-1]
                    chunk_last_ev = evs_c[-1]
                    chunk_last_ev2 = None
                    if "qkv" in dbg or "vv" in dbg:
                        ap = dbg_aps.get("dbg_v") or dbg_out("dbg_v", [T, E])
                        o = kb.op("scalar", lambda eng, par=par: eng.activation(
                            qevb[par][:], ps_qkv[par][:], AF.Copy),
                            f"p3dve{ci}", [ev_mm, evs_c[-1], qb_free[par]])
                        ev_dv = kb.inc(o, sem_a)
                        ps_free[par] = ev_dv
                        o, evd = kb.dma("gpsimd",
                                        lambda a=ap, t=t, fo=fo: a[_ts(t), (fo - 8) * 512:(fo - 8) * 512 + 512],
                                        lambda par=par: qevb[par][:], f"d3v{ci}",
                                        waits=[ev_dv], ch="dbg")
                        qb_free[par] = evd
                ci += 1
            flush_tp()
            if fo < 4:
                tgt_hi, tgt_lo = s_q_hi, s_q_lo
                hb = fo * 4
                key = "q"
            elif fo < 8:
                tgt_hi, tgt_lo = s_k_hi, s_k_lo
                hb = (fo - 4) * 4
                key = "k"
            else:
                tgt_hi, tgt_lo = s_v_hi, s_v_lo
                hb = (fo - 8) * 4
                key = "v"
            st_evs = []
            for hh in range(4):
                for stg, tgt in ((stg3_hi, tgt_hi), (stg3_lo, tgt_lo)):
                    _, ev_s = kb.dma(
                        "gpsimd",
                        lambda tgt=tgt, hb=hb, hh=hh, half=half:
                            tgt[hb + hh, :, half * 1024:(half + 1) * 1024],
                        lambda stg=stg, hh=hh: stg[:, hh * 1024:(hh + 1) * 1024],
                        f"p3st{half}_{fo}_{hh}",
                        waits=[chunk_last_ev, chunk_last_ev2],
                        ch=f"st3{hh}")
                    st_evs.append(ev_s)
            stg_free3 = [st_evs[i * 2 + 1] for i in range(4)]
            for hh in range(4):
                ev_qkv_store[key][(hb + hh, half)] = st_evs[hh * 2 + 1]
        lhsT_free = last_mm
    p3_done = stg_free3 + [last_mm, wsplit_done]
    es3.close()

    es4 = ExitStack()
    e4 = es4.enter_context
    qs_hi = [e4(nc.sbuf_tensor(f"qs_hi{i}", [128, T], BF16)) for i in range(2)]
    qs_lo = [e4(nc.sbuf_tensor(f"qs_lo{i}", [128, T], BF16)) for i in range(2)]
    ks_hi = [e4(nc.sbuf_tensor(f"ks_hi{i}", [128, T], BF16)) for i in range(2)]
    ks_lo = [e4(nc.sbuf_tensor(f"ks_lo{i}", [128, T], BF16)) for i in range(2)]
    vs_hi = [e4(nc.sbuf_tensor(f"vs_hi{i}", [128, T], BF16)) for i in range(2)]
    vs_lo = [e4(nc.sbuf_tensor(f"vs_lo{i}", [128, T], BF16)) for i in range(2)]
    s2b = [e4(nc.sbuf_tensor(f"s2b{i}", [128, 512], F32)) for i in range(2)]
    afb = [e4(nc.sbuf_tensor(f"afb{i}", [128, 512], F32)) for i in range(2)]
    ahib = [e4(nc.sbuf_tensor(f"ahib{i}", [128, 512], BF16)) for i in range(2)]
    alob = [e4(nc.sbuf_tensor(f"alob{i}", [128, 512], BF16)) for i in range(2)]
    rdv = e4(nc.sbuf_tensor("rdv", [1, 512], F32))
    rdbf = e4(nc.sbuf_tensor("rdbf", [128, 512], F32))
    tfb = e4(nc.sbuf_tensor("tfb", [128, 512], F32))
    at_hi_stg = e4(nc.sbuf_tensor("at_hi_stg", [128, T], BF16))
    at_lo_stg = e4(nc.sbuf_tensor("at_lo_stg", [128, T], BF16))
    ps_s = [e4(nc.psum_tensor(f"ps_s{i}", [128, 512], F32)) for i in range(3)]
    ps_o = [e4(nc.psum_tensor(f"ps_o{i}", [128, 512], F32)) for i in range(2)]
    ps_d = e4(nc.psum_tensor("ps_d", [128, 512], F32))
    ps_b = e4(nc.psum_tensor("ps_b", [128, 512], F32))

    CHUNKS = []
    for c in range(4):
        kmax = min(TT - 1, 4 * c + 3)
        for kb_ in range(kmax + 1):
            q0 = max(512 * c, 128 * kb_)
            CHUNKS.append((c, kb_, q0, 512 * (c + 1)))

    strip_free = [None, None]
    sq_free = [None, None, None]
    s2_free = [None, None]
    af_free = [None, None]
    a_free = [None, None]
    po_free = [None, None]
    pd_free = None
    pb_free = None
    tf_free = None
    stg4_free = None
    dbg_at_free = None
    ev_p4_store = []
    if "attn" in dbg:
        dbg_out("dbg_attn", [E, T])

    for h in range(H):
        hb = h % 2
        ld_evs = []
        ldw = [p3_done if h == 0 else [strip_free[hb]]][0]
        for nm2, bufs, srcs in (("qh", qs_hi, s_q_hi), ("ql", qs_lo, s_q_lo),
                                ("kh", ks_hi, s_k_hi), ("kl", ks_lo, s_k_lo),
                                ("vh", vs_hi, s_v_hi), ("vl", vs_lo, s_v_lo)):
            w4 = list(ldw)
            if h < 2:
                kk = nm2[0]
                w4 += [ev_qkv_store[kk][(h, 0)], ev_qkv_store[kk][(h, 1)]]
            _, ev_l = kb.dma("sync", lambda b=bufs, hb=hb: b[hb][:],
                             lambda s=srcs, h=h: s[h], f"p4l{h}{nm2}",
                             waits=w4, ch="p4" + nm2)
            ld_evs.append(ev_l)

        ev_alo_c = {}
        ev_av_last = None
        ev_dmm = None

        def emit_S(idx):
            c, kb_, q0, q1 = CHUNKS[idx]
            w = q1 - q0
            sp = idx % 3
            par = idx % 2
            mmw = ld_evs + [sq_free[sp]]
            o = None
            for ic, (lt, rt) in enumerate(((ks_hi[hb], qs_hi[hb]),
                                           (ks_hi[hb], qs_lo[hb]),
                                           (ks_lo[hb], qs_hi[hb]))):
                o = kb.op("tensor", lambda eng, lt=lt, rt=rt, kb_=kb_, q0=q0,
                          q1=q1, sp=sp, ic=ic: eng.matmul(
                              ps_s[sp][:, 0:q1 - q0], lt[:, _ts(kb_)],
                              rt[:, q0:q1], start=(ic == 0), stop=(ic == 2)),
                          f"p4s{h}_{idx}_{ic}", mmw if ic == 0 else ())
            ev_mm = kb.inc(o, sem_t)
            o = kb.op("scalar", lambda eng, w=w, sp=sp, par=par: eng.activation(
                s2b[par][:, 0:w], ps_s[sp][:, 0:w], AF.Square),
                f"p4sq{h}_{idx}", [ev_mm, s2_free[par]])
            ev_sq = kb.inc(o, sem_a)
            sq_free[sp] = ev_sq
            o = kb.op("vector", lambda eng, w=w, par=par: eng.tensor_tensor(
                afb[par][:, 0:w], s2b[par][:, 0:w], s2b[par][:, 0:w], ALU.mult),
                f"p4a4{h}_{idx}", [ev_sq, af_free[par]])
            ev = kb.inc(o, sem_v)
            s2_free[par] = ev
            if q0 == kb_ * 128:
                o = kb.op("vector", lambda eng, par=par: eng.tensor_tensor(
                    afb[par][:, 0:128], afb[par][:, 0:128], trim[:], ALU.mult),
                    f"p4msk{h}_{idx}", [ev, ev_const["trim"]])
                ev = kb.inc(o, sem_v)
            o = kb.op("scalar", lambda eng, w=w, par=par: eng.activation(
                ahib[par][:, 0:w], afb[par][:, 0:w], AF.Copy),
                f"p4ah{h}_{idx}", [ev, a_free[par],
                                   stg4_free if idx < 2 and h > 0 else None])
            ev_hi = kb.inc(o, sem_a)
            af_free[par] = ev_hi
            o = kb.op("vector", lambda eng, w=w, par=par: eng.tensor_tensor(
                alob[par][:, 0:w], afb[par][:, 0:w], ahib[par][:, 0:w],
                ALU.subtract), f"p4al{h}_{idx}", [ev_hi])
            ev_alo_c[idx] = kb.inc(o, sem_v)

        def emit_av(idx):
            nonlocal ev_av_last, ev_dmm
            c, kb_, q0, q1 = CHUNKS[idx]
            w = q1 - q0
            par = idx % 2
            kmax = min(TT - 1, 4 * c + 3)
            off = q0 - 512 * c
            wv = [ev_alo_c.pop(idx)]
            if kb_ == 0:
                wv.append(po_free[c % 2])
            o = None
            for ic, (lt, rt) in enumerate(((vs_hi[hb], ahib[par]),
                                           (vs_hi[hb], alob[par]),
                                           (vs_lo[hb], ahib[par]))):
                o = kb.op("tensor", lambda eng, lt=lt, rt=rt, kb_=kb_, off=off,
                          w=w, ic=ic, c=c, kmax=kmax: eng.matmul(
                              ps_o[c % 2][:, off:off + w], lt[:, _ts(kb_)],
                              rt[:, 0:w], start=(kb_ == 0 and ic == 0),
                              stop=(kb_ == kmax and ic == 2)),
                          f"p4o{h}_{idx}_{ic}", wv if ic == 0 else ())
            ev_av_last = kb.inc(o, sem_t)
            dw = []
            if kb_ == 0:
                dw.append(pd_free)
            for ic, rt in enumerate((ahib[par], alob[par])):
                o = kb.op("tensor", lambda eng, rt=rt, off=off, w=w, ic=ic,
                          kb_=kb_, kmax=kmax: eng.matmul(
                              ps_d[0:1, off:off + w], ones_k[:], rt[:, 0:w],
                              start=(kb_ == 0 and ic == 0),
                              stop=(kb_ == kmax and ic == 1)),
                          f"p4d{h}_{idx}_{ic}",
                          dw if ic == 0 else ())
            ev_dmm = kb.inc(o, sem_t)
            a_free[par] = ev_dmm

        def emit_tail(c):
            nonlocal pd_free, pb_free, tf_free, dbg_at_free, stg4_free
            nonlocal last_stgV, last_stgA
            o = kb.op("vector", lambda eng: eng.reciprocal(
                rdv[:], ps_d[0:1, :]), f"p4r{h}_{c}", [ev_dmm])
            pd_free = ev_recip = kb.inc(o, sem_v)
            o = kb.op("tensor", lambda eng: eng.matmul(
                ps_b[:, 0:512], ones_row[0:1, :], rdv[0:1, :],
                start=True, stop=True), f"p4b{h}_{c}",
                [ev_recip, ev_const["ones_row"], pb_free])
            ev_b = kb.inc(o, sem_t)
            o = kb.op("vector", lambda eng: eng.tensor_copy(
                rdbf[:], ps_b[:, 0:512]), f"p4be{h}_{c}", [ev_b])
            pb_free = kb.inc(o, sem_v)
            o = kb.op("vector", lambda eng, c=c: eng.tensor_tensor(
                tfb[:], ps_o[c % 2][:, :], rdbf[:], ALU.mult),
                f"p4n{h}_{c}", [ev_av_last, pb_free, tf_free, dbg_at_free])
            ev_tf = kb.inc(o, sem_v)
            po_free[c % 2] = ev_tf
            o = kb.op("scalar", lambda eng, c=c: eng.activation(
                at_hi_stg[:, _ts(c, 512)], tfb[:], AF.Copy), f"p4nh{h}_{c}",
                [ev_tf, stg4_free if c == 0 else None])
            tf_free = ev_hi2 = kb.inc(o, sem_a)
            last_stgA = ev_hi2
            o = kb.op("vector", lambda eng, c=c: eng.tensor_tensor(
                at_lo_stg[:, _ts(c, 512)], tfb[:], at_hi_stg[:, _ts(c, 512)],
                ALU.subtract), f"p4nl{h}_{c}", [ev_hi2])
            last_stgV = kb.inc(o, sem_v)
            if "attn" in dbg:
                ap = dbg_aps["dbg_attn"]
                _, dbg_at_free = kb.dma(
                    "gpsimd", lambda a=ap, h=h, c=c: a[_ts(h), _ts(c, 512)],
                    lambda: tfb[:], f"d4{h}_{c}", waits=[ev_tf], ch="dbg")

        last_stgV = None
        last_stgA = None
        emit_S(0)
        if len(CHUNKS) > 1:
            emit_S(1)
        for idx in range(len(CHUNKS)):
            if idx + 2 < len(CHUNKS):
                emit_S(idx + 2)
            emit_av(idx)
            c, kb_, q0, q1 = CHUNKS[idx]
            if idx + 1 == len(CHUNKS) or CHUNKS[idx + 1][0] != c:
                emit_tail(c)

        st4 = []
        for stg, tgt, chn in ((at_hi_stg, s_at_hi, "p4sth"),
                              (at_lo_stg, s_at_lo, "p4stl")):
            _, ev_s = kb.dma(
                "gpsimd",
                lambda tgt=tgt, h=h: tgt[:, :, _ts(h)].transpose([1, 0, 2]),
                lambda stg=stg: stg[:], f"p4st{h}",
                waits=[last_stgV, last_stgA], ch=chn)
            st4.append(ev_s)
        stg4_free = st4[-1]
        ev_p4_store.append(st4)
        strip_free[hb] = ev_av_last

    es4.close()
    p4_done = [stg4_free, ev_av_last, pb_free]

    es4b = ExitStack()
    e4b = es4b.enter_context
    wos = e4b(nc.sbuf_tensor("wos", [128, 1024], F32))
    wo_hi = e4b(nc.sbuf_tensor("wo_hi", [128, 16 * 1024], BF16))
    wo_lo = e4b(nc.sbuf_tensor("wo_lo", [128, 16 * 1024], BF16))
    at4h = [e4b(nc.sbuf_tensor(f"at4h{i}", [128, E], BF16)) for i in range(2)]
    at4l = [e4b(nc.sbuf_tensor(f"at4l{i}", [128, E], BF16)) for i in range(2)]
    yab = [e4b(nc.sbuf_tensor(f"yab{i}", [128, 1024], F32)) for i in range(2)]
    ps_y = [e4b(nc.psum_tensor(f"ps_y{i}", [128, 1024], F32)) for i in range(3)]
    q_hcb = e4b(nc.sbuf_tensor("q_hcb", [128, 2048], F32))
    q_z = e4b(nc.sbuf_tensor("q_z", [128, 2048], F32))
    q_u = e4b(nc.sbuf_tensor("q_u", [128, 2048], F32))
    q_c = e4b(nc.sbuf_tensor("q_c", [128, 2048], F32))
    q_hqtm = e4b(nc.sbuf_tensor("q_hqtm", [128, FI], BF16))
    q_stg = e4b(nc.sbuf_tensor("q_stg", [128, NIC * 128], BF16))
    ps_th = [e4b(nc.psum_tensor(f"ps_th{i}", [128, 1024], BF16)) for i in range(2)]

    ev_hq_store = [None] * TT
    hqs = {"hcb_free": None, "nop": None, "tp_free": [None, None],
           "stg_free": None, "stg_free_hi": None, "hqtm_free": None,
           "zero": None, "peel": []}

    def _hq_zero(t=None):
        o = kb.op("vector", lambda eng: eng.memset(
            q_stg[64:128, (NIC - 1) * 128:NIC * 128], 0.0), "hqz", p4_done)
        hqs["zero"] = kb.inc(o, sem_v)

    def _hq_load(t, cc):
        c0 = cc * 2048
        c1 = min(FI, c0 + 2048)
        w = ([ev_h_store[-1][1], ev_h_store[-2][1]] if (t == 0 and cc == 0) else []) \
            + ([hqs["hcb_free"]] if hqs["hcb_free"] else [])
        _, ev = kb.dma("sync", lambda c0=c0, c1=c1: q_hcb[:, 0:c1 - c0],
                       lambda t=t, c0=c0, c1=c1: s_h[_ts(t), c0:c1],
                       f"hql{t}_{cc}", waits=w, ch="lh5")
        hqs["load"] = ev

    def _hq_z(t, cc):
        c0 = cc * 2048
        c1 = min(FI, c0 + 2048)
        o = kb.op("scalar", lambda eng, t=t, c0=c0, c1=c1: eng.activation(
            q_z[:, 0:c1 - c0], q_hcb[:, 0:c1 - c0], AF.Copy, bias=128.5,
            scale=sxh_all[:, t:t + 1]), f"hqz{t}_{cc}",
            [hqs["load"], ev_sxh, hqs["nop"]])
        hqs["zev"] = kb.inc(o, sem_a)
        hqs["hcb_free"] = hqs["zev"]

    def _hq_peel(t, cc):
        c0 = cc * 2048
        c1 = min(FI, c0 + 2048)
        ew = [hqs["hqtm_free"]] if (cc == 0 and hqs["hqtm_free"]) else []
        done = emit_peel(
            kb, (sem_v, sem_a, sem_g), q_z, q_u, q_c,
            lambda o0, w2, c0=c0: q_hqtm[:, c0 + o0:c0 + o0 + w2],
            0, c1 - c0, hqs["zev"], f"hqp{t}_{cc}_", extra_waits=ew,
            bias_m128=(biasm128, ev_const["bm128"]))
        o = kb.op("vector", lambda eng: eng.engine_nop(), f"hqn{t}_{cc}", done)
        hqs["nop"] = kb.inc(o, sem_v)
        hqs["peel"].append(hqs["nop"])

    def _hq_tp(t, icg):
        last = None
        for ic in range(icg * 4, min(NIC, icg * 4 + 4)):
            wic = min(128, FI - ic * 128)
            tpp = ic % 2
            stw = hqs["stg_free"] if ic == 0 else \
                hqs["stg_free_hi"] if ic == 24 else None
            o = kb.op("tensor", lambda eng, ic=ic, wic=wic, tpp=tpp:
                      eng.transpose(ps_th[tpp][:wic, 0:128],
                                    q_hqtm[:, ic * 128:ic * 128 + wic],
                                    identb[:]),
                      f"hqt{t}_{ic}",
                      hqs["peel"] + [hqs["tp_free"][tpp]] +
                      ([stw] if stw else []))
            ev_tp = kb.inc(o, sem_t)
            last = ev_tp
            o = kb.op("vector", lambda eng, ic=ic, wic=wic, tpp=tpp:
                      eng.tensor_copy(q_stg[:wic, ic * 128:(ic + 1) * 128],
                                      ps_th[tpp][:wic, 0:128]),
                      f"hqe{t}_{ic}", [ev_tp])
            hqs["tp_free"][tpp] = kb.inc(o, sem_v)
        if icg == 12:
            hqs["hqtm_free"] = last
            hqs["peel"] = []

    def _hq_store_lo(t):
        _, ev = kb.dma("gpsimd", lambda t=t: s_hq[t, :, 0:24 * 128],
                       lambda: q_stg[:, 0:24 * 128], f"hqsl{t}",
                       waits=[hqs["tp_free"][0], hqs["tp_free"][1]],
                       ch="shq")
        hqs["stg_free"] = ev

    def _hq_store_hi(t):
        _, ev = kb.dma("gpsimd", lambda t=t: s_hq[t, :, 24 * 128:NIC * 128],
                       lambda: q_stg[:, 24 * 128:NIC * 128], f"hqsh{t}",
                       waits=[hqs["tp_free"][0], hqs["tp_free"][1], hqs["zero"]],
                       ch="shq")
        ev_hq_store[t] = ev
        hqs["stg_free_hi"] = ev

    hq_steps = [lambda t=None: _hq_zero()]
    for t in range(TT):
        for cc in range(4):
            hq_steps.append(lambda t=t, cc=cc: _hq_load(t, cc))
            hq_steps.append(lambda t=t, cc=cc: _hq_z(t, cc))
            hq_steps.append(lambda t=t, cc=cc: _hq_peel(t, cc))
        for icg in range(13):
            hq_steps.append(lambda t=t, icg=icg: _hq_tp(t, icg))
            if icg == 5:
                hq_steps.append(lambda t=t: _hq_store_lo(t))
        hq_steps.append(lambda t=t: _hq_store_hi(t))
    hq_state_i = [0]

    def hq_pump(n):
        for _ in range(n):
            if hq_state_i[0] >= len(hq_steps):
                return
            hq_steps[hq_state_i[0]]()
            hq_state_i[0] += 1

    at_free = [None, None]
    ya_free = [None, None]
    psy_free = [None, None, None]
    ev_ya_store = []
    wsplit4_done = None
    for foh in (0, 1):
        last_wsp = None
        for Kc in range(16):
            w4b = p4_done if (foh == 0 and Kc == 0) else \
                ([wsplit4_done] if Kc == 0 else [])
            _, ev_w = kb.dma("sync", lambda Kc=Kc, foh=foh:
                             wos[:], lambda Kc=Kc, foh=foh:
                             w_out[_ts(Kc), foh * 1024:(foh + 1) * 1024],
                             f"p4bw{foh}_{Kc}", waits=w4b + [last_wsp] if last_wsp else w4b,
                             ch="wo5")
            o = kb.op("vector", lambda eng, Kc=Kc: eng.tensor_copy(
                wo_hi[:, Kc * 1024:(Kc + 1) * 1024], wos[:]),
                f"p4bh{foh}_{Kc}", [ev_w])
            ev = kb.inc(o, sem_v)
            o = kb.op("vector", lambda eng, Kc=Kc: eng.tensor_tensor(
                wo_lo[:, Kc * 1024:(Kc + 1) * 1024], wos[:],
                wo_hi[:, Kc * 1024:(Kc + 1) * 1024], ALU.subtract),
                f"p4bl{foh}_{Kc}", [ev])
            last_wsp = kb.inc(o, sem_v)
        wsplit4_done = last_wsp
        for t in range(TT):
            par = t % 3
            apar = t % 2
            wld = [ev_p4_store[-1][0], ev_p4_store[-1][1]] if (foh == 0 and t < 2) else []
            _, evh = kb.dma("sync", lambda par=apar: at4h[par][:],
                            lambda t=t: s_at_hi[t], f"p4bah{foh}_{t}",
                            waits=wld + ([at_free[apar]] if at_free[apar] else []),
                            ch="at4h")
            _, evl = kb.dma("sync", lambda par=apar: at4l[par][:],
                            lambda t=t: s_at_lo[t], f"p4bal{foh}_{t}",
                            waits=wld + ([at_free[apar]] if at_free[apar] else []),
                            ch="at4l")
            ev_mm4 = None
            mw = [evh, evl, wsplit4_done, psy_free[par]]
            for Kc in range(16):
                for ic, (lt, rt) in enumerate(((at4h[apar], wo_hi),
                                               (at4h[apar], wo_lo),
                                               (at4l[apar], wo_hi))):
                    for half2 in range(2):
                        o = kb.op("tensor", lambda eng, lt=lt, rt=rt, Kc=Kc,
                                  par=par, h2=half2, ic=ic: eng.matmul(
                                      ps_y[par][:, h2 * 512:(h2 + 1) * 512],
                                      lt[:, _ts(Kc)],
                                      rt[:, Kc * 1024 + h2 * 512:Kc * 1024 + (h2 + 1) * 512],
                                      start=(Kc == 0 and ic == 0),
                                      stop=(Kc == 15 and ic == 2)),
                                  f"p4bm{foh}_{t}_{Kc}_{ic}_{half2}",
                                  mw if (Kc == 0 and ic == 0 and half2 == 0) else ())
                        if Kc == 15 and ic == 2 and half2 == 1:
                            ev_mm4 = kb.inc(o, sem_t)
                if not _HQ_NOPUMP:
                    hq_pump(1)
            at_free[apar] = ev_mm4
            o = kb.op("vector", lambda eng, par=par, apar=apar: eng.tensor_copy(
                yab[apar][:], ps_y[par][:]), f"p4bev{foh}_{t}",
                [ev_mm4] + ([ya_free[apar]] if ya_free[apar] else []))
            psy_free[par] = ev_ev4 = kb.inc(o, sem_v)
            _, ev_st = kb.dma("gpsimd",
                              lambda t=t, foh=foh: s_ya[_ts(t), foh * 1024:(foh + 1) * 1024],
                              lambda par=apar: yab[par][:], f"p4bst{foh}_{t}",
                              waits=[ev_ev4], ch="ya")
            ya_free[apar] = ev_st
            ev_ya_store.append(ev_st)
    hq_pump(len(hq_steps))
    p4b_done = [ev_ya_store[-1], psy_free[0], psy_free[1], psy_free[2],
                at_free[0], at_free[1], ev_hq_store[-1],
                hqs["tp_free"][0], hqs["tp_free"][1], hqs["nop"]]
    es4b.close()

    if "ya" in dbg:
        ap = dbg_out("dbg_ya", [T, E])
        prevd = None
        for t in range(TT):
            _, prevd = kb.dma("gpsimd", lambda a=ap, t=t: a[_ts(t), :],
                              lambda t=t: s_ya[_ts(t), :], f"dya{t}",
                              waits=[ev_ya_store[-1]] + ([prevd] if prevd else []),
                              ch="dbg")

    es5b = ExitStack()
    e5b = es5b.enter_context
    wdchk = e5b(nc.sbuf_tensor("wdchk", [128, E], F32))
    absb2 = e5b(nc.sbuf_tensor("absb2", [128, E], F32))
    sgnb2 = e5b(nc.sbuf_tensor("sgnb2", [128, E], F32))
    mskb2 = e5b(nc.sbuf_tensor("mskb2", [128, E], F32))
    wqdtm = e5b(nc.sbuf_tensor("wqdtm", [128, FI], BF16))
    wdT = e5b(nc.sbuf_tensor("wdT", [128, NIC * 512], BF16))
    hqf = [e5b(nc.sbuf_tensor(f"hqf{i}", [128, NIC * 128], BF16)) for i in range(2)]
    yb5 = e5b(nc.sbuf_tensor("yb5", [128, 512], F32))
    xb5 = [e5b(nc.sbuf_tensor(f"xb5{i}", [128, 512], F32)) for i in range(2)]
    yab5 = [e5b(nc.sbuf_tensor(f"yab5{i}", [128, 512], F32)) for i in range(2)]
    ob5 = [e5b(nc.sbuf_tensor(f"ob5{i}", [128, 512], F32)) for i in range(2)]
    cdx = cdxp
    ps_y5 = [e5b(nc.psum_tensor(f"ps_y5{i}", [128, 512], F32)) for i in range(2)]
    ps_t5b = [e5b(nc.psum_tensor(f"ps_t5b{i}", [128, 1024], BF16)) for i in range(2)]
    prev_sgn5b = None
    prev_wq5b = None
    hrow_free = None
    prev_zq1 = None
    hq_free = [None, None]
    ob_free = [None, None]
    x5_free = [None, None]
    ya5_free = [None, None]
    y5_free = [None, None]
    yb5_free = None
    ev_hq_store = [None] * TT
    last_out = []
    last_dn_mm = None
    tp5b_free = [None, None]
    ev_wd_done = None
    for foq in range(4):
        for s in range(4):
            r0 = foq * 512 + s * 128
            for cc in range(4):
                c0 = cc * 2048
                c1 = min(FI, c0 + 2048)
                wld = [last_dn_mm] if (s == 0 and cc == 0 and last_dn_mm) else []
                if foq == 0 and s == 0 and cc == 0:
                    wld += p4b_done
                wld += [prev_sgn5b]
                _, ev_w = kb.dma("sync", lambda c0=c0, c1=c1: wdchk[:, 0:c1 - c0],
                                 lambda r0=r0, c0=c0, c1=c1: w_d[r0:r0 + 128, c0:c1],
                                 f"p5bw{foq}_{s}_{cc}", waits=wld, ch="w5b")
                o = kb.op("scalar", lambda eng, c0=c0, c1=c1: eng.activation(
                    absb2[:, 0:c1 - c0], wdchk[:, 0:c1 - c0], AF.Abs,
                    scale=wq_sc["d"]["sw"][:]), f"p5bab{foq}_{s}_{cc}",
                    [ev_w, ev_wsc[("d", "sw")] if foq == 0 else None,
                     prev_wq5b])
                ev_a5 = kb.inc(o, sem_a)
                o = kb.op("scalar", lambda eng, c0=c0, c1=c1: eng.activation(
                    sgnb2[:, 0:c1 - c0], wdchk[:, 0:c1 - c0], AF.Sign),
                    f"p5bsg{foq}_{s}_{cc}")
                ev_s5 = prev_sgn5b = kb.inc(o, sem_a)
                o = kb.op("vector", lambda eng, c0=c0, c1=c1: eng.tensor_scalar(
                    mskb2[:, 0:c1 - c0], absb2[:, 0:c1 - c0], 0.5, None,
                    ALU.is_ge), f"p5bm{foq}_{s}_{cc}", [ev_a5])
                ev = kb.inc(o, sem_v)
                o = kb.op("vector", lambda eng, c0=c0, c1=c1: eng.tensor_tensor(
                    wqdtm[:, c0:c1], mskb2[:, 0:c1 - c0], sgnb2[:, 0:c1 - c0],
                    ALU.mult), f"p5bq{foq}_{s}_{cc}", [ev, ev_s5])
                ev_q5 = prev_wq5b = kb.inc(o, sem_v)
            for ic in range(NIC):
                wic = min(128, FI - ic * 128)
                tpp = ic % 2
                o = kb.op("tensor", lambda eng, ic=ic, wic=wic, tpp=tpp:
                          eng.transpose(ps_t5b[tpp][:wic, 0:128],
                                        wqdtm[:, ic * 128:ic * 128 + wic],
                                        identb[:]),
                          f"p5bt{foq}_{s}_{ic}",
                          [ev_q5, ev_const["identb"], tp5b_free[tpp]])
                ev_tp = kb.inc(o, sem_t)
                o = kb.op("vector", lambda eng, ic=ic, s=s, wic=wic, tpp=tpp:
                          eng.tensor_copy(
                              wdT[:wic, ic * 512 + s * 128:ic * 512 + s * 128 + 128],
                              ps_t5b[tpp][:wic, 0:128]),
                          f"p5bte{foq}_{s}_{ic}", [ev_tp])
                tp5b_free[tpp] = kb.inc(o, sem_v)
                ev_wd_done = tp5b_free[tpp]
        for t in range(TT):
            par = t % 2
            _, ev_hql = kb.dma("sync", lambda par=par: hqf[par][:, 0:NIC * 128],
                               lambda t=t: s_hq[t], f"p5blq{foq}_{t}",
                               waits=[ev_hq_store[t]] +
                               ([hq_free[par]] if hq_free[par] else []),
                               ch="lhq")
            ev_hq_ready = ev_hql
            o = None
            mw5 = [ev_hq_ready, ev_wd_done, y5_free[par]]
            for ic in range(NIC):
                wic = min(128, FI - ic * 128)
                o = kb.op("tensor", lambda eng, ic=ic, wic=wic, par=par:
                          eng.matmul(ps_y5[par][:],
                                     hqf[par][:wic, _ts(ic)],
                                     wdT[:wic, ic * 512:(ic + 1) * 512],
                                     start=(ic == 0), stop=(ic == NIC - 1)),
                          f"p5bdm{foq}_{t}_{ic}", mw5 if ic == 0 else ())
            last_dn_mm = ev_dn = kb.inc(o, sem_t)
            hq_free[par] = ev_dn
            _, ev_x5 = kb.dma("sync", lambda par=par: xb5[par][:],
                              lambda t=t, foq=foq: x_in[_ts(t), _ts(foq, 512)],
                              f"p5blx{foq}_{t}",
                              waits=[x5_free[par]] if x5_free[par] else (),
                              ch="lx5")
            _, ev_ya5 = kb.dma("sync", lambda par=par: yab5[par][:],
                               lambda t=t, foq=foq: s_ya[_ts(t), _ts(foq, 512)],
                               f"p5blya{foq}_{t}",
                               waits=([ev_ya_store[-1]] if (foq == 0 and t < 2) else []) +
                               ([ya5_free[par]] if ya5_free[par] else []),
                               ch="lya")
            o = kb.op("scalar", lambda eng, par=par, t=t: eng.activation(
                yb5[:], ps_y5[par][:], AF.Copy, scale=cdx[:, t:t + 1]),
                f"p5bev{foq}_{t}", [ev_dn, ev_cdx, y5_free[par], yb5_free])
            ev_y = kb.inc(o, sem_a)
            y5_free[par] = ev_y
            o = kb.op("vector", lambda eng, par=par: eng.tensor_tensor(
                ob5[par][:], yb5[:], yab5[par][:], ALU.add), f"p5ba1{foq}_{t}",
                [ev_y, ev_ya5, ob_free[par]])
            ev = yb5_free = kb.inc(o, sem_v)
            ya5_free[par] = ev
            o = kb.op("vector", lambda eng, par=par: eng.tensor_tensor(
                ob5[par][:], ob5[par][:], xb5[par][:], ALU.add), f"p5ba2{foq}_{t}",
                [ev, ev_x5])
            ev_o5 = kb.inc(o, sem_v)
            x5_free[par] = ev_o5
            _, ev_out = kb.dma("gpsimd",
                               lambda t=t, foq=foq: out[_ts(t), _ts(foq, 512)],
                               lambda par=par: ob5[par][:], f"p5bo{foq}_{t}",
                               waits=[ev_o5], ch=f"oo{par}")
            ob_free[par] = ev_out
            last_out.append(ev_out)

    fin_evs = [last_out[-1], last_out[-2], ev_p1_done]
    o = kb.op("gpsimd", lambda eng: eng.engine_nop(), "fin",
              [ev for ev in fin_evs if ev is not None])
    kb.inc(o, sem_g)
    es5b.close()

    kb.emit()
    es.close()
    return nc



_LAST_RESULTS = None


def kernel(L0_attn_x, L0_attn_w_qkv, L0_attn_w_out, L0_attn_ln_scale,
           L0_attn_ln_bias, L0_ffn_w_gate, L0_ffn_w_up, L0_ffn_w_down):
    global _LAST_RESULTS
    debug = tuple(sorted(k for k, v in _DEBUG.items() if v))
    nc = build_nc(debug=debug)
    consts = host_constants()
    x = np.ascontiguousarray(np.asarray(L0_attn_x, dtype=np.float32))
    base = {
        "w_qkv": np.ascontiguousarray(np.asarray(L0_attn_w_qkv, np.float32)),
        "w_out": np.ascontiguousarray(np.asarray(L0_attn_w_out, np.float32)),
        "ln_scale_b": np.ascontiguousarray(
            np.broadcast_to(np.asarray(L0_attn_ln_scale, np.float32)[None, :],
                            (128, 128))),
        "ln_bias_b": np.ascontiguousarray(
            np.broadcast_to(np.asarray(L0_attn_ln_bias, np.float32)[None, :],
                            (128, 128))),
        "w_gate": np.ascontiguousarray(np.asarray(L0_ffn_w_gate, np.float32)),
        "w_up": np.ascontiguousarray(np.asarray(L0_ffn_w_up, np.float32)),
        "w_down": np.ascontiguousarray(np.asarray(L0_ffn_w_down, np.float32)),
        "cos_hi": consts["cos_hi"], "cos_lo": consts["cos_lo"],
        "sin_hi": consts["sin_hi"], "sin_lo": consts["sin_lo"],
        "ident_f32": consts["ident_f32"], "ident_bf16": consts["ident_bf16"],
        "trimask": consts["trimask"],
    }
    in_maps = [dict(base, x=x[b]) for b in range(NC)]
    res = run_bass_kernel_spmd(nc, in_maps, core_ids=list(range(NC)))
    _LAST_RESULTS = res.results
    return np.stack([res.results[b]["out"] for b in range(NC)], axis=0)
